# version 12
# baseline (speedup 1.0000x reference)
import numpy as np
import concourse.bass as bass
import concourse.mybir as mybir
from concourse.bass_utils import run_bass_kernel_spmd

F32 = mybir.dt.float32
BF16 = mybir.dt.bfloat16
AF = mybir.ActivationFunctionType
ALU = mybir.AluOpType
AX = mybir.AxisListType

NCORES = 8
TOK = 2048
D = 1024
FF = 4096
NTC = 16
NDC = 8
R_ROUNDS = 21
SCALE_D2 = 2.0 / 4096.0
EPS = 1e-5

KW_BOT = 209715
KW_TOP = 4194304 - 209715 + 1
KTARGETS = [KW_BOT, KW_TOP, 204, 4096 - 204 + 1,
            KW_BOT, KW_TOP, 51, 1024 - 51 + 1]

DBG = True


class Ctr:
    __slots__ = ("sem", "n")

    def __init__(self, sem):
        self.sem = sem
        self.n = 0


class DQ:

    def __init__(self, sems):
        self.sems = sems
        self.i = 0
        self.last = None

    def all(self):
        return [(s, s.n) for s in self.sems]


STOP_AT = None
MARKS = {}


def build_nc(stop_at=None):
    nc = bass.Bass("TRN2", target_bir_lowering=False, debug=False)

    def inp(name, shape, dt=F32):
        return nc.dram_tensor(name, list(shape), dt, kind="ExternalInput").ap()

    x_tm = inp("x_tm", (TOK, D))
    wkT = inp("wkT", (D, D))
    wvT = inp("wvT", (D, D))
    wqT = inp("wqT", (D, D))
    w0T = inp("w0T", (D, D))
    w1T = inp("w1T", (D, D))
    w1s = inp("w1s", (D, D))
    woutT = inp("woutT", (D, D))
    wg2 = inp("wg2", (D, 2))
    bg2 = inp("bg2", (1, 2))
    b1g_i = inp("b1g", (128, 32))
    b2g_i = inp("b2g", (128, 8))
    gamg_i = inp("gamg", (128, 8))
    betg_i = inp("betg", (128, 8))
    wr_i = inp("wr", (D, 8))
    br_i = inp("br", (1, 8))
    w1sl = inp("w1sl", (128, FF))
    w2sl = inp("w2sl", (128, FF))
    w1ff = inp("w1ff", (D, FF))
    w2ff = inp("w2ff", (FF, D))
    idb_i = inp("idb", (128, 128), BF16)

    out_tm = nc.dram_tensor("out_tm", [TOK, D], F32, kind="ExternalOutput").ap()

    def scr(name, shape, dt=F32):
        return nc.dram_tensor(name, list(shape), dt)

    q_dram = scr("q_dram", (D, TOK), BF16)
    ln0blk = scr("ln0blk", (NDC, TOK, 128), BF16)
    ublk = scr("ublk", (NDC, TOK, 128), BF16)
    y1_dram = scr("y1_dram", (TOK, D))
    absw1_d = scr("absw1_d", (128, FF))
    absw2_d = scr("absw2_d", (128, FF))
    g1_part = scr("g1_part", (D, D))
    g1_sum = scr("g1_sum", (D, D))
    g0_part = scr("g0_part", (D, D))
    g0_sum = scr("g0_sum", (D, D))
    gate_in = scr("gate_in", (1, 2))
    gate_out = scr("gate_out", (1, 2))
    gate_bc = scr("gate_bc", (1, 2))
    cc_in = scr("cc_in", (128, 1))
    cc_out = scr("cc_out", (128, 1))
    cc_in2 = scr("cc_in2", (128, 1))
    cc_out2 = scr("cc_out2", (128, 1))
    tr_d = scr("tr_d", (1, 8))
    tb_d = scr("tb_d", (1, 8))
    blA_d = scr("blA_d", (128, 1))
    blB_d = scr("blB_d", (128, 1))
    thr_d = scr("thr_d", (1, 8))
    nthr_d = scr("nthr_d", (1, 8))
    cbc_d = scr("cbc_d", (1, 16))
    pcd1 = scr("pcd1", (1, TOK), BF16)
    pcd2 = scr("pcd2", (1, TOK), BF16)
    pcc_d = scr("pcc_d", (128, 16))
    ppc_d = scr("ppc_d", (128, 16))
    w1t_loc = scr("w1t_loc", (256, 128, 128), BF16)
    w2bf_loc = scr("w2bf_loc", (FF, D), BF16)
    w1p_in = scr("w1p_in", (32, 128, 128), BF16)
    w1p_d = scr("w1p_d", (256, 128, 128), BF16)
    pw2_in = scr("pw2_in", (512, D), BF16)
    pw2_d = scr("pw2_d", (FF, D), BF16)

    dbg = {}
    if DBG:
        def dout(name, shape, dt=F32):
            dbg[name] = nc.dram_tensor(name, list(shape), dt, kind="ExternalOutput").ap()
        dout("dbg_gates", (1, 4))
        dout("dbg_thr", (1, 8))
        dout("dbg_k", (128, NDC * TOK), BF16)
        dout("dbg_d2", (128, NDC * TOK), BF16)
        dout("dbg_w0nT", (128, NDC * D), BF16)
        dout("dbg_attn", (128, NDC * TOK), BF16)
        dout("dbg_pc", (2, TOK), BF16)
        dout("dbg_actc", (128, 16384), BF16)
        dout("dbg_actp", (128, 16384), BF16)
        dout("dbg_yc", (128, 8192), BF16)
        dout("dbg_fo", (128, 4096), BF16)

    from contextlib import ExitStack
    ctx = ExitStack()

    def sbt(name, shape, dt=F32):
        return ctx.enter_context(nc.sbuf_tensor(name, list(shape), dt))

    slots_f = [sbt(f"slot{i}", [128, 8192], F32) for i in range(5)]
    slots = [s[:, :].bitcast(BF16) for s in slots_f]
    wa_f = sbt("warena", [128, 8192], F32)
    wa_b = wa_f[:, :].bitcast(BF16)
    sbig_f = sbt("sbig", [128, 2048], F32)
    sbig_b = sbig_f[:, :].bitcast(BF16)

    idb = sbt("idb_sb", [128, 128], BF16)
    cnt_p = sbt("cnt_p", [128, 32], F32)
    cnt_t = sbt("cnt_t", [128, 32], F32)
    cnt_r = sbt("cnt_r", [128, 1], F32)
    Lrow = sbt("Lrow", [1, 128], F32)
    Crow = sbt("Crow", [1, 32], F32)
    cnt8 = sbt("cnt8", [1, 8], F32)
    lo8 = sbt("lo8", [1, 8], F32)
    hi8 = sbt("hi8", [1, 8], F32)
    t8a = sbt("t8a", [1, 8], F32)
    t8b = sbt("t8b", [1, 8], F32)
    ktar = sbt("ktar", [1, 8], F32)
    thrb = sbt("thrb", [128, 8], F32)
    thrf = sbt("thrf", [128, 8], F32)
    nthrf = sbt("nthrf", [128, 8], F32)
    mx_p = sbt("mx_p", [128, 32], F32)
    cnt_t2 = sbt("cnt_t2", [128, 32], F32)
    cnt_pb = sbt("cnt_pb", [128, 32], F32)
    cnt_rb = sbt("cnt_rb", [128, 1], F32)
    cnt_rb2 = sbt("cnt_rb2", [128, 1], F32)
    bCrow = sbt("bCrow", [1, 32], F32)
    bCrow2 = sbt("bCrow2", [1, 32], F32)
    tr8 = sbt("tr8", [1, 8], F32)
    tb8 = sbt("tb8", [1, 8], F32)
    thrt = sbt("thrt", [128, 8], F32)
    thtb = sbt("thtb", [128, 8], F32)
    t8c = sbt("t8c", [1, 8], F32)
    t8d = sbt("t8d", [1, 8], F32)
    t8c2 = sbt("t8c2", [1, 8], F32)
    t8d2 = sbt("t8d2", [1, 8], F32)
    cnt_r2 = sbt("cnt_r2", [128, 1], F32)
    Lrow2 = sbt("Lrow2", [1, 128], F32)
    Crow2 = sbt("Crow2", [1, 32], F32)
    cnt82 = sbt("cnt82", [1, 8], F32)
    t8a2 = sbt("t8a2", [1, 8], F32)
    t8b2 = sbt("t8b2", [1, 8], F32)
    b1gs = sbt("b1gs", [128, 32], F32)
    b2gs = sbt("b2gs", [128, 8], F32)
    b1a = sbt("b1a", [128, 32], F32)
    b2a = sbt("b2a", [128, 8], F32)
    pb1g = sbt("pb1g", [128, 32], F32)
    pb2g = sbt("pb2g", [128, 8], F32)
    tb1 = sbt("tb1", [128, 32], F32)
    tb2 = sbt("tb2", [128, 32], F32)
    wgb = sbt("wgb", [128, 16], BF16)
    wgf = sbt("wgf", [128, 16], F32)
    bg_sb = sbt("bg_sb", [1, 2], F32)
    gacc = sbt("gacc", [2, 4], F32)
    grow = sbt("grow", [1, 2], F32)
    gate2 = sbt("gate2", [1, 2], F32)
    gate4 = sbt("gate4", [1, 4], F32)
    gal = sbt("gal", [128, 2], F32)
    stat = sbt("stat", [128, 16], F32)
    stat2 = sbt("stat2", [128, 16], F32)
    mcol = sbt("mcol", [128, 16], F32)
    vcol = sbt("vcol", [128, 16], F32)
    rcol = sbt("rcol", [128, 16], F32)
    muU = sbt("muU", [128, 16], F32)
    rU = sbt("rU", [128, 16], F32)
    rmU = sbt("rmU", [128, 16], F32)
    wr_sb = sbt("wr_sb", [128, 64], F32)
    wrp = sbt("wrp", [128, 64], BF16)
    wrb = sbt("wrb", [128, 64], BF16)
    gam_sb = sbt("gam_sb", [128, 8], F32)
    bet_sb = sbt("bet_sb", [128, 8], F32)
    br_sb = sbt("br_sb", [1, 8], F32)
    c1r = sbt("c1r", [1, 8], F32)
    c2r = sbt("c2r", [1, 8], F32)
    c12bc = sbt("c12bc", [128, 16], F32)
    lgt = sbt("lgt", [128, 8], F32)
    lg2 = sbt("lg2", [128, 8], F32)
    lg3 = sbt("lg3", [128, 8], F32)
    expv = sbt("expv", [128, 8], F32)
    sm = [sbt(f"sm{i}", [128, 1], F32) for i in range(6)]
    pcc = sbt("pcc", [128, 16], F32)
    ppc = sbt("ppc", [128, 16], F32)
    onesb = sbt("onesb", [128, 1], BF16)
    epsb = sbt("epsb", [128, 1], F32)
    trash = sbt("trash", [128, 512], BF16)

    pm = [ctx.enter_context(nc.psum_tensor(f"pm{i}", [128, 512], F32)) for i in range(4)]
    pt = [ctx.enter_context(nc.psum_tensor(f"pt{i}", [128, 1024], BF16)) for i in range(2)]
    px = ctx.enter_context(nc.psum_tensor("px", [128, 512], F32))
    pr = ctx.enter_context(nc.psum_tensor("pr", [128, 512], F32))

    SD = DQ([Ctr(ctx.enter_context(nc.semaphore(f"SD{i}"))) for i in range(6)])
    SC = DQ([Ctr(ctx.enter_context(nc.semaphore(f"SC{i}"))) for i in range(6)])
    GD = DQ([Ctr(ctx.enter_context(nc.semaphore(f"GD{i}"))) for i in range(2)])
    MM = Ctr(ctx.enter_context(nc.semaphore("MM")))
    VE = Ctr(ctx.enter_context(nc.semaphore("VE")))
    SE = Ctr(ctx.enter_context(nc.semaphore("SE")))
    CC = Ctr(ctx.enter_context(nc.semaphore("CC")))

    plan = {k: [] for k in ("sync", "scalar", "vector", "tensor", "gpsimd")}
    seqc = [0]
    marks = {}

    def mark(name):
        marks[name] = seqc[0]

    def E(eng, build=None, waits=(), inc=None, inc_k=1):
        wl = [(c.sem, v) for (c, v) in waits]
        tok = None
        if isinstance(inc, DQ) and build is not None:
            s = inc.sems[inc.i % len(inc.sems)]
            inc.i += 1
            wl = [(s.sem, s.n)] + wl
            s.n += 16
            tok = (s, s.n)
            inc.last = tok
            sem, inc_k = s.sem, 16
        else:
            sem = inc.sem if inc is not None else None

        def fn(e, _w=wl, _b=build, _s=sem, _k=inc_k):
            for (s_, v) in _w:
                e.wait_ge(s_, v)
            if _b is not None:
                ins = _b(e)
                if _s is not None:
                    ins.then_inc(_s, _k)

        fn._seq = seqc[0]
        seqc[0] += 1
        plan[eng].append(fn)
        if tok is not None:
            return tok
        if inc is not None and build is not None:
            inc.n += inc_k
            return inc.n
        return None

    bank_free = {}

    def bwait(bank):
        w = bank_free.get(id(bank))
        return [w] if w is not None else []

    def bset(bank, ctr, val):
        bank_free[id(bank)] = (ctr, val)

    def fm(si, c, c0, n):
        return slots[si][:, c * TOK + c0: c * TOK + c0 + n]

    def tm(si, c, c0, n):
        return slots[si][:, c * D + c0: c * D + c0 + n]

    tok_idb = E("sync", lambda e: e.dma_start(out=idb[:], in_=idb_i), inc=SD)
    tok_b1g = E("sync", lambda e: e.dma_start(out=b1gs[:], in_=b1g_i), inc=SD)
    tok_b2g = E("sync", lambda e: e.dma_start(out=b2gs[:], in_=b2g_i), inc=SD)
    tok_bg = E("sync", lambda e: e.dma_start(out=bg_sb[:], in_=bg2), inc=SD)
    tok_wg = E("sync", lambda e: e.dma_start(
        out=wgf[:].rearrange("p (c t) -> p c t", c=8),
        in_=wg2.rearrange("(c p) t -> p c t", p=128)), inc=SD)
    E("scalar", lambda e: e.copy(wgb[:], wgf[:]), waits=[tok_wg], inc=SE)
    se_wgb = SE.n
    sd1 = E("sync", lambda e: e.dma_start(out=wa_f[:, 0:4096], in_=w1sl), inc=SD)
    E("scalar", lambda e: e.activation(wa_f[:, 4096:8192], wa_f[:, 0:4096], AF.Abs),
      waits=[sd1], inc=SE)
    se1 = SE.n
    E("vector", lambda e: e.reduce_max(mx_p[:, 0:1], wa_f[:, 4096:8192], axis=AX.X),
      waits=[(SE, se1)], inc=VE)
    E("vector", lambda e: e.reduce_max(mx_p[:, 1:2], wa_f[:, 4096:8192], axis=AX.X), inc=VE)
    ve_w1m = VE.n
    sc_a1 = E("scalar", lambda e: e.dma_start(out=absw1_d.ap(), in_=wa_f[:, 4096:8192]),
      waits=[(SE, se1)], inc=SC)
    sd2 = E("sync", lambda e: e.dma_start(out=wa_f[:, 0:4096], in_=w2sl),
      waits=[sc_a1, (VE, ve_w1m)], inc=SD)
    E("scalar", lambda e: e.activation(wa_f[:, 4096:8192], wa_f[:, 0:4096], AF.Abs),
      waits=[sd2, sc_a1], inc=SE)
    se2 = SE.n
    E("vector", lambda e: e.reduce_max(mx_p[:, 4:5], wa_f[:, 4096:8192], axis=AX.X),
      waits=[(SE, se2)], inc=VE)
    E("vector", lambda e: e.reduce_max(mx_p[:, 5:6], wa_f[:, 4096:8192], axis=AX.X), inc=VE)
    ve_w2m = VE.n
    sc_a2 = E("scalar", lambda e: e.dma_start(out=absw2_d.ap(), in_=wa_f[:, 4096:8192]),
      waits=[(SE, se2)], inc=SC)

    E("scalar", lambda e: e.activation(b1a[:], b1gs[:], AF.Abs), waits=[tok_b1g, tok_b2g, tok_idb, tok_bg], inc=SE)
    E("scalar", lambda e: e.activation(b2a[:], b2gs[:], AF.Abs), inc=SE)
    se_babs = SE.n

    def _init(e):
        for i, kv in enumerate(KTARGETS):
            e.memset(ktar[0:1, i:i + 1], float(kv))
        e.memset(epsb[:], EPS)
        return e.memset(lo8[:], 0.0)
    E("vector", _init, inc=VE)
    E("vector", lambda e: e.memset(cnt_p[:], 0.0), inc=VE)
    E("vector", lambda e: e.reduce_max(mx_p[:, 2:3], b1a[:], axis=AX.X),
      waits=[(SE, se_babs)], inc=VE)
    E("vector", lambda e: e.reduce_max(mx_p[:, 3:4], b1a[:], axis=AX.X), inc=VE)
    E("vector", lambda e: e.reduce_max(mx_p[:, 6:7], b2a[:], axis=AX.X), inc=VE)
    E("vector", lambda e: e.reduce_max(mx_p[:, 7:8], b2a[:], axis=AX.X), inc=VE)
    E("vector", lambda e: e.memset(mx_p[:, 8:32], 0.0), inc=VE)
    ve_mx = VE.n
    E("vector", lambda e: e.transpose(cnt_t[:], mx_p[:]), waits=[(VE, ve_mx)], inc=VE)
    v = VE.n
    E("vector", lambda e: e.reduce_max(cnt_r[:], cnt_t[:], axis=AX.X), waits=[(VE, v)], inc=VE)
    ve_r = VE.n
    sc0 = E("scalar", lambda e: e.dma_start(out=cc_in.ap(), in_=cnt_r[:]),
      waits=[(VE, ve_r)], inc=SC)
    E("gpsimd", lambda e: e.collective_compute(
        "AllReduce", ALU.max, replica_groups=[list(range(NCORES))],
        ins=[cc_in.ap().opt()], outs=[cc_out.ap().opt()]),
      waits=[sc0], inc=CC)
    cc_hi = CC.n
    sd_L = E("sync", lambda e: e.dma_start(out=Lrow[:], in_=cc_out.ap().rearrange("p a -> a p")),
      waits=[(CC, cc_hi)], inc=SD)
    E("vector", lambda e: e.reduce_max(
        hi8[:], Lrow[:].rearrange("a (r i) -> a i r", r=4)[:, 0:8, :], axis=AX.X),
      waits=[sd_L], inc=VE)
    v = VE.n
    for (dst, sl, f) in ((tr8, (0, 0, 2), 1 / 3), (tr8, (2, 0, 2), 2 / 3),
                         (tr8, (4, 4, 6), 1 / 3), (tr8, (6, 4, 6), 2 / 3),
                         (tb8, (0, 2, 4), 1 / 3), (tb8, (2, 2, 4), 2 / 3),
                         (tb8, (4, 6, 8), 1 / 3), (tb8, (6, 6, 8), 2 / 3)):
        E("vector", lambda e, dst=dst, sl=sl, f=f: e.tensor_scalar(
            dst[0:1, sl[0]:sl[0] + 2], hi8[0:1, sl[1]:sl[2]], f, None, op0=ALU.mult),
          waits=[(VE, v)], inc=VE)
    E("vector", lambda e: e.memset(cnt_pb[:], 0.0), inc=VE)
    ve_mid0 = VE.n

    sstate = {"buf0": 0, "buf1": 0, "ve_mid": ve_mid0}
    pump_state = {"gen": None, "acc": 0.0, "done": False, "spacing": 11.0}

    def pump(w=2.0):
        ps = pump_state
        if ps["done"] or ps["gen"] is None:
            return
        ps["acc"] += w
        while ps["acc"] >= ps["spacing"] and not ps["done"]:
            try:
                next(ps["gen"])
                ps["acc"] -= ps["spacing"]
            except StopIteration:
                ps["done"] = True

    def pump_flush():
        ps = pump_state
        if ps["gen"] is None:
            return
        while not ps["done"]:
            try:
                next(ps["gen"])
            except StopIteration:
                ps["done"] = True

    def emit_thr_bcast(src_row, dst, ve_ready):
        prevs = sstate.get("thrd_prev") or []
        s = E("gpsimd", lambda e: e.dma_start(out=thr_d.ap(), in_=src_row[:]),
              waits=[(VE, ve_ready)] + prevs, inc=GD)
        tok = E("gpsimd", lambda e: e.dma_start(
            out=dst[:], in_=thr_d.ap().broadcast_to([128, 8])),
          waits=[s], inc=GD)
        sstate["thrd_prev"] = [tok]
        return tok

    GA = {"c0": 0, "gb": 0, "src": absw1_d, "barr": b1a, "bw": 32,
          "bufs": (sbig_f[:, 0:512], sbig_f[:, 512:1024]),
          "cp": cnt_p, "cnt_t": cnt_t, "cnt_r": cnt_r,
          "cc_in": cc_in, "cc_out": cc_out, "Lrow": Lrow, "Crow": Crow,
          "cnt8": cnt8, "t8a": t8a, "t8b": t8b, "t8c": t8c, "t8d": t8d,
          "cnt_tb": cnt_t, "cnt_rb": cnt_rb, "bl_d": blA_d, "BLrow": Lrow,
          "bCrow": bCrow,
          "ve_buf": [0, 0], "thrd_prev": None, "tbd_prev": None,
          "sc_thrt": None, "sc_thtb": None, "pre": None}
    GB = {"c0": 4, "gb": 4, "src": absw2_d, "barr": b2a, "bw": 8,
          "bufs": (sbig_f[:, 1024:1536], sbig_f[:, 1536:2048]),
          "cp": mx_p, "cnt_t": cnt_t2, "cnt_r": cnt_r2,
          "cc_in": cc_in2, "cc_out": cc_out2, "Lrow": Lrow2, "Crow": Crow2,
          "cnt8": cnt82, "t8a": t8a2, "t8b": t8b2, "t8c": t8c2, "t8d": t8d2,
          "cnt_tb": cnt_t2, "cnt_rb": cnt_rb2, "bl_d": blB_d, "BLrow": Lrow2,
          "bCrow": bCrow2,
          "ve_buf": [0, 0], "thrd_prev": None, "tbd_prev": None,
          "sc_thrt": None, "sc_thtb": None, "pre": None}

    def g_bcastx(G, key, srcrow, dram, dst, ve_ready):
        gb = G["gb"]
        prev = G[key]
        s = E("gpsimd", lambda e, gb=gb, srcrow=srcrow, dram=dram: e.dma_start(
            out=dram.ap()[0:1, gb:gb + 4], in_=srcrow[0:1, gb:gb + 4]),
          waits=[(VE, ve_ready)] + ([prev] if prev else []), inc=GD)
        tok = E("gpsimd", lambda e, gb=gb, dram=dram, dst=dst: e.dma_start(
            out=dst[:, gb:gb + 4],
            in_=dram.ap()[0:1, gb:gb + 4].broadcast_to([128, 4])),
          waits=[s], inc=GD)
        G[key] = tok
        return tok

    def g_prefetch(G, waits=()):
        G["pre"] = E("sync", lambda e, s=G["src"], b=G["bufs"][0]: e.dma_start(
            out=b, in_=s.ap()[:, 0:512]),
          waits=[(VE, G["ve_buf"][0])] + list(waits), inc=SD)

    def g_select(G, lo, hi, t1, t2, c1v, c2v, kt, tnew1, tnew2):
        ta, tb_ = G["t8c"], G["t8d"]
        b1, b2, m1, m2 = ta[0:1, 0:2], ta[0:1, 2:4], ta[0:1, 4:6], ta[0:1, 6:8]
        u1, u2, u3, sc = tb_[0:1, 0:2], tb_[0:1, 2:4], tb_[0:1, 4:6], tb_[0:1, 6:8]
        ops = [
            lambda e: e.tensor_tensor(b1, c1v, kt, op=ALU.is_ge),
            lambda e: e.tensor_tensor(b2, c2v, kt, op=ALU.is_ge),
            lambda e: e.tensor_sub(m1, b2, b1),
            lambda e: e.tensor_scalar(m2, b2, -1.0, 1.0, op0=ALU.mult, op1=ALU.add),
            lambda e: e.tensor_mul(u1, lo, b1),
            lambda e: e.tensor_mul(u2, t1, m1),
            lambda e: e.tensor_mul(u3, t2, m2),
            lambda e: e.tensor_add(u1, u1, u2),
            lambda e: e.tensor_mul(u2, t1, b1),
            lambda e: e.tensor_add(u1, u1, u3),
            lambda e: e.tensor_mul(u3, t2, m1),
            lambda e: e.tensor_mul(sc, hi, m2),
            lambda e: e.tensor_add(u2, u2, u3),
            lambda e: e.tensor_add(hi, u2, sc),
            lambda e: e.tensor_copy(lo, u1),
            lambda e: e.tensor_sub(sc, hi, lo),
            lambda e: e.tensor_scalar(sc, sc, 1.0 / 3.0, None, op0=ALU.mult),
            lambda e: e.tensor_add(tnew1, lo, sc),
            lambda e: e.tensor_add(tnew2, tnew1, sc),
        ]
        for op in ops:
            E("vector", op, waits=[(VE, VE.n)], inc=VE)

    def g_round(G):
        c0, gb = G["c0"], G["gb"]
        sc_thrt = G["sc_thrt"]
        sc_thtb = G["sc_thtb"]
        toks = [None] * 8
        toks[0] = G["pre"]
        toks[1] = E("sync", lambda e, s=G["src"], b=G["bufs"][1]: e.dma_start(
            out=b, in_=s.ap()[:, 512:1024]),
          waits=[(VE, G["ve_buf"][1])], inc=SD)
        for c in range(8):
            buf = G["bufs"][c % 2]
            for sq in range(4):
                col = sq * 8 + c
                thc = gb + sq
                E("vector", lambda e, b=buf, col=col, thc=thc, cp=G["cp"]: e.tensor_scalar(
                    trash[:, 0:512], b, thrt[:, thc:thc + 1], None,
                    op0=ALU.is_le, op1=ALU.add,
                    accum_out=cp[:, col:col + 1]),
                  waits=[toks[c], sc_thrt, (VE, VE.n)], inc=VE)
            G["ve_buf"][c % 2] = VE.n
            if c + 2 < 8:
                toks[c + 2] = E("sync", lambda e, s=G["src"], c=c, b=buf: e.dma_start(
                    out=b, in_=s.ap()[:, (c + 2) * 512:(c + 3) * 512]),
                  waits=[(VE, G["ve_buf"][c % 2])], inc=SD)
        for sq in range(4):
            colb = gb + sq
            thc = gb + sq
            E("vector", lambda e, arr=G["barr"], colb=colb, thc=thc, w=G["bw"]: e.tensor_scalar(
                trash[:, 0:w], arr[:], thtb[:, thc:thc + 1], None,
                op0=ALU.is_le, op1=ALU.add, accum_out=cnt_pb[:, colb:colb + 1]),
              waits=[sc_thtb, (VE, VE.n)], inc=VE)
        ve_cmp = VE.n
        E("vector", lambda e, t=G["cnt_t"], cp=G["cp"]: e.transpose(t[:], cp[:]),
          waits=[(VE, ve_cmp)], inc=VE)
        E("vector", lambda e, t=G["cnt_t"], r=G["cnt_r"]: e.reduce_sum(r[:], t[:], axis=AX.X),
          waits=[(VE, VE.n)], inc=VE)
        v2 = VE.n
        s1 = E("gpsimd", lambda e, ci=G["cc_in"], r=G["cnt_r"]: e.dma_start(out=ci.ap(), in_=r[:]),
          waits=[(VE, v2)], inc=GD)
        E("vector", lambda e, t=G["cnt_tb"]: e.transpose(t[:], cnt_pb[:]),
          waits=[(VE, VE.n)], inc=VE)
        E("vector", lambda e, t=G["cnt_tb"], r=G["cnt_rb"]: e.reduce_sum(r[:], t[:], axis=AX.X),
          waits=[(VE, VE.n)], inc=VE)
        vb2 = VE.n
        sb1 = E("gpsimd", lambda e, d=G["bl_d"], r=G["cnt_rb"]: e.dma_start(
            out=d.ap(), in_=r[:]),
          waits=[(VE, vb2)], inc=GD)
        g_prefetch(G)
        E("gpsimd", lambda e, ci=G["cc_in"], co=G["cc_out"]: e.collective_compute(
            "AllReduce", ALU.add, replica_groups=[list(range(NCORES))],
            ins=[ci.ap().opt()], outs=[co.ap().opt()]),
          waits=[s1], inc=CC)
        c1 = CC.n
        yield
        sbn = E("gpsimd", lambda e, d=G["bl_d"], L=G["BLrow"]: e.dma_start(
            out=L[:], in_=d.ap().rearrange("p a -> a p")),
          waits=[sb1, (VE, VE.n)], inc=GD)
        E("vector", lambda e, L=G["BLrow"], C=G["bCrow"]: e.reduce_sum(
            C[:], L[:].rearrange("a (r i) -> a i r", r=4), axis=AX.X),
          waits=[sbn, (VE, VE.n)], inc=VE)
        g_select(G,
                 lo8[0:1, c0 + 2:c0 + 4], hi8[0:1, c0 + 2:c0 + 4],
                 tb8[0:1, gb:gb + 2], tb8[0:1, gb + 2:gb + 4],
                 G["bCrow"][0:1, gb:gb + 2], G["bCrow"][0:1, gb + 2:gb + 4],
                 ktar[0:1, c0 + 2:c0 + 4],
                 tb8[0:1, gb:gb + 2], tb8[0:1, gb + 2:gb + 4])
        G["sc_thtb"] = g_bcastx(G, "tbd_prev", tb8, tb_d, thtb, VE.n)
        sdn = E("gpsimd", lambda e, L=G["Lrow"], co=G["cc_out"]: e.dma_start(
            out=L[:], in_=co.ap().rearrange("p a -> a p")),
          waits=[(CC, c1), (VE, VE.n)], inc=GD)
        E("vector", lambda e, L=G["Lrow"], C=G["Crow"]: e.reduce_sum(
            C[:], L[:].rearrange("a (r i) -> a i r", r=4), axis=AX.X),
          waits=[sdn, (VE, VE.n)], inc=VE)
        E("vector", lambda e, C=G["Crow"], n8=G["cnt8"]: e.reduce_sum(
            n8[0:1, 0:4], C[:, 0:32].rearrange("a (k c) -> a k c", c=8), axis=AX.X),
          waits=[(VE, VE.n)], inc=VE)
        g_select(G,
                 lo8[0:1, c0:c0 + 2], hi8[0:1, c0:c0 + 2],
                 tr8[0:1, gb:gb + 2], tr8[0:1, gb + 2:gb + 4],
                 G["cnt8"][0:1, 0:2], G["cnt8"][0:1, 2:4],
                 ktar[0:1, c0:c0 + 2],
                 tr8[0:1, gb:gb + 2], tr8[0:1, gb + 2:gb + 4])
        G["sc_thrt"] = g_bcastx(G, "thrd_prev", tr8, tr_d, thrt, VE.n)

    def emit_search_extract():
        sstate["sc_thrb"] = emit_thr_bcast(hi8, thrb, VE.n)
        yield
        E("vector", lambda e: e.memset(mx_p[:], 0.0),
          waits=[sstate["sc_thrb"]] + sstate.get("pre_toks", []), inc=VE)
        for a, src in enumerate((absw1_d, absw2_d)):
            for c in range(4):
                bload = sbig_f[:, 0:1024]
                bmask = sbig_f[:, 1024:2048]
                sd_c = E("gpsimd", lambda e, s=src, c=c, b=bload: e.dma_start(
                    out=b, in_=s.ap()[:, c * 1024:(c + 1) * 1024]),
                  waits=[(VE, VE.n)] + sstate.get("pre_toks", []), inc=GD)
                for ti in range(2):
                    col = a * 8 + ti * 4 + c
                    thc = a * 4 + ti
                    E("vector", lambda e, b=bload, m=bmask, thc=thc: e.scalar_tensor_tensor(
                        m, b, thrb[:, thc:thc + 1], b, op0=ALU.is_le, op1=ALU.mult),
                      waits=[sd_c, (VE, VE.n)], inc=VE)
                    v1 = VE.n
                    E("vector", lambda e, m=bmask, col=col: e.reduce_max(
                        mx_p[:, col:col + 1], m, axis=AX.X),
                      waits=[(VE, v1)], inc=VE)
                yield
        for arr, col, thc in ((b1a, 16, 2), (b1a, 17, 3), (b2a, 18, 6), (b2a, 19, 7)):
            E("vector", lambda e, arr=arr, thc=thc, w=arr.shape[1]: e.scalar_tensor_tensor(
                tb1[:, 0:w], arr[:], thrb[:, thc:thc + 1], arr[:],
                op0=ALU.is_le, op1=ALU.mult),
              waits=[(VE, VE.n)], inc=VE)
            v1 = VE.n
            E("vector", lambda e, col=col, w=arr.shape[1]: e.reduce_max(
                mx_p[:, col:col + 1], tb1[:, 0:w], axis=AX.X),
              waits=[(VE, v1)], inc=VE)
        ve_xm = VE.n
        E("vector", lambda e: e.transpose(cnt_t[:], mx_p[:]), waits=[(VE, ve_xm)], inc=VE)
        v1 = VE.n
        E("vector", lambda e: e.reduce_max(cnt_r[:], cnt_t[:], axis=AX.X),
          waits=[(VE, v1)], inc=VE)
        v2 = VE.n
        s1 = E("gpsimd", lambda e: e.dma_start(out=cc_in.ap(), in_=cnt_r[:]),
          waits=[(VE, v2)], inc=GD)
        E("gpsimd", lambda e: e.collective_compute(
            "AllReduce", ALU.max, replica_groups=[list(range(NCORES))],
            ins=[cc_in.ap().opt()], outs=[cc_out.ap().opt()]),
          waits=[s1], inc=CC)
        c1 = CC.n
        sdn = E("gpsimd", lambda e: e.dma_start(out=Lrow[:], in_=cc_out.ap().rearrange("p a -> a p")),
          waits=[(CC, c1)], inc=GD)
        E("vector", lambda e: e.reduce_max(
            Crow[:], Lrow[:].rearrange("a (r i) -> a i r", r=4), axis=AX.X),
          waits=[sdn, (VE, VE.n)], inc=VE)
        v3 = VE.n
        E("vector", lambda e: e.reduce_max(
            t8a[0:1, 0:4], Crow[:, 0:16].rearrange("a (s c) -> a s c", c=4), axis=AX.X),
          waits=[(VE, v3)], inc=VE)
        v4 = VE.n
        E("vector", lambda e: e.tensor_copy(t8a[0:1, 4:8], Crow[:, 16:20]),
          waits=[(VE, v4)], inc=VE)
        ve_thr = VE.n
        s2 = E("gpsimd", lambda e: e.dma_start(out=thr_d.ap(), in_=t8a[:]),
          waits=[(VE, ve_thr)], inc=GD)
        tf1 = E("gpsimd", lambda e: e.dma_start(out=thrf[:], in_=thr_d.ap().broadcast_to([128, 8])),
          waits=[s2], inc=GD)
        E("vector", lambda e: e.tensor_scalar(t8b[:], t8a[:], -1.0, None, op0=ALU.mult),
          waits=[(VE, ve_thr)], inc=VE)
        vn = VE.n
        s3 = E("gpsimd", lambda e: e.dma_start(out=nthr_d.ap(), in_=t8b[:]),
          waits=[(VE, vn)], inc=GD)
        tf2 = E("gpsimd", lambda e: e.dma_start(out=nthrf[:], in_=nthr_d.ap().broadcast_to([128, 8])),
          waits=[s3], inc=GD)
        sstate["sc_thrf"] = [tf1, tf2]
        if DBG:
            E("scalar", lambda e: e.dma_start(out=dbg["dbg_thr"], in_=t8a[:]),
              waits=[(VE, ve_thr)], inc=SC)

    mark('ph0')
    xb = [wa_f[:, 0:1024], wa_f[:, 1024:2048]]
    sqb = wa_f[:, 2048:3072]
    sd_x = {}
    ve_sq_done = 0
    se_prev_sq = 0
    for c in range(NTC):
        buf = xb[c % 2]
        wv = sd_x.get(f"used{c % 2}", 0)
        sdc = E("sync", lambda e, c=c, b=buf: e.dma_start(
            out=b, in_=x_tm[c * 128:(c + 1) * 128, :]),
          waits=[(VE, wv), sc_a2], inc=SD)
        E("vector", lambda e, b=buf, c=c: e.reduce_sum(stat[:, c:c + 1], b, axis=AX.X),
          waits=[sdc], inc=VE)
        E("scalar", lambda e, b=buf: e.activation(sqb, b, AF.Square),
          waits=[sdc, (VE, ve_sq_done)], inc=SE)
        sec = SE.n
        E("vector", lambda e, c=c: e.reduce_sum(stat2[:, c:c + 1], sqb, axis=AX.X),
          waits=[(SE, sec)], inc=VE)
        ve_sq_done = VE.n
        sd_x[f"used{c % 2}"] = VE.n

    def stats_chain(ve_in, s1t, s2t, mdst, rdst):
        E("vector", lambda e: e.tensor_scalar(mdst[:], s1t[:], 1.0 / D, None, op0=ALU.mult),
          waits=[(VE, ve_in)], inc=VE)
        v1 = VE.n
        E("vector", lambda e: e.tensor_mul(vcol[:], mdst[:], mdst[:]),
          waits=[(VE, v1)], inc=VE)
        v2 = VE.n
        E("vector", lambda e: e.scalar_tensor_tensor(
            vcol[:], s2t[:], 1.0 / D, vcol[:], op0=ALU.mult, op1=ALU.subtract),
          waits=[(VE, v2)], inc=VE)
        v3 = VE.n
        E("scalar", lambda e: e.activation(rdst[:], vcol[:], AF.Sqrt, bias=epsb[:]),
          waits=[(VE, v3)], inc=SE)
        s1 = SE.n
        E("vector", lambda e: e.reciprocal(rdst[:], rdst[:]), waits=[(SE, s1)], inc=VE)
        return VE.n

    ve_ln0stats = stats_chain(VE.n, stat, stat2, mcol, rcol)

    lnb = [wa_b[:, 6144:7168], wa_b[:, 7168:8192]]
    sc_ln0 = 0
    for c in range(NTC):
        buf = xb[c % 2]
        wv = sd_x.get(f"u2{c % 2}", 0)
        sdc = E("sync", lambda e, c=c, b=buf: e.dma_start(
            out=b, in_=x_tm[c * 128:(c + 1) * 128, :]),
          waits=[(VE, wv), (VE, ve_ln0stats)], inc=SD)
        ob = lnb[c % 2]
        wsc = sd_x.get(f"lnb{c % 2}")
        E("vector", lambda e, b=buf, o=ob, c=c: e.tensor_scalar(
            o, b, mcol[:, c:c + 1], rcol[:, c:c + 1], op0=ALU.subtract, op1=ALU.mult),
          waits=[sdc] + ([wsc] if wsc else []), inc=VE)
        vc = VE.n
        sd_x[f"u2{c % 2}"] = vc
        sd_x[f"lnb{c % 2}"] = E("scalar", lambda e, o=ob, c=c: e.dma_start(
            out=ln0blk.ap()[:, c * 128:(c + 1) * 128, :].rearrange("g p d -> p g d"),
            in_=o.rearrange("p (g d) -> p g d", g=8)),
          waits=[(VE, vc)], inc=SC)
    sc_ln0 = SC.all()

    mark('A')
    for dc in range(NDC):
        E("scalar", lambda e, dc=dc: e.dma_start(
            out=fm(0, dc, 0, TOK), in_=ln0blk.ap()[dc], transpose=True),
          waits=sc_ln0, inc=SC)
    sc_ln0fm = SC.all()

    cast_stage = [slots_f[3][:, 0:1024], slots_f[3][:, 1024:2048]]
    cast_stage2 = [slots_f[2][:, 1536:2560], slots_f[2][:, 2560:3584]]
    cast_state = {"s0": 0, "s1": 0}

    def cast_weight(src, dst_bf, pre_waits=(), stage=None):
        stg = stage if stage is not None else cast_stage
        for c in range(NDC):
            st = stg[c % 2]
            wv = cast_state[f"s{c % 2}"]
            sdc = E("sync", lambda e, c=c, s=st, src=src: e.dma_start(
                out=s, in_=src[c * 128:(c + 1) * 128, :]),
              waits=[(SE, wv)] + list(pre_waits), inc=SD)
            E("scalar", lambda e, c=c, s=st, d=dst_bf: e.copy(
                d[:, c * 1024:(c + 1) * 1024], s),
              waits=[sdc], inc=SE)
            cast_state[f"s{c % 2}"] = SE.n
            pump(1.2)
        return SE.n

    RW0 = wa_b[:, 0:8192]
    RW1 = wa_b[:, 8192:16384]
    se_wk = cast_weight(wkT, RW0)
    se_wv = cast_weight(wvT, RW1)

    ve_gacc = 0
    for ts4 in range(4):
        w = bwait(px)
        for dc in range(NDC):
            E("tensor", lambda e, dc=dc, ts4=ts4: e.matmul(
                px[0:2, :], wgb[:, dc * 2:dc * 2 + 2], fm(0, dc, ts4 * 512, 512),
                start=(dc == 0), stop=(dc == 7)),
              waits=((sc_ln0fm + [(SE, se_wgb)]) if (ts4 == 0 and dc == 0) else []) + (w if dc == 0 else []),
              inc=(MM if dc == 7 else None))
        mmg = MM.n
        E("vector", lambda e, ts4=ts4: e.reduce_sum(gacc[:, ts4:ts4 + 1], px[0:2, :], axis=AX.X),
          waits=[(MM, mmg), (VE, ve_gacc)], inc=VE)
        ve_gacc = VE.n
        bset(px, VE, ve_gacc)
    E("vector", lambda e: e.reduce_sum(sm[0][0:2, :], gacc[0:2, :], axis=AX.X),
      waits=[(VE, ve_gacc)], inc=VE)
    ve_gpart = VE.n
    sc_gin = E("scalar", lambda e: e.dma_start(out=gate_in.ap().rearrange("a t -> t a"), in_=sm[0][0:2, :]),
      waits=[(VE, ve_gpart)], inc=SC)
    E("gpsimd", lambda e: e.collective_compute(
        "AllReduce", ALU.add, replica_groups=[list(range(NCORES))],
        ins=[gate_in.ap().opt()], outs=[gate_out.ap().opt()]),
      waits=[sc_gin], inc=CC)
    cc_gate = CC.n
    sd_gate = E("sync", lambda e: e.dma_start(out=grow[:], in_=gate_out.ap()),
      waits=[(CC, cc_gate)], inc=SD)
    E("vector", lambda e: e.tensor_scalar(grow[:], grow[:], 1.0 / 16384.0, None, op0=ALU.mult),
      waits=[sd_gate], inc=VE)
    v1 = VE.n
    E("vector", lambda e: e.tensor_add(grow[:], grow[:], bg_sb[:]), waits=[(VE, v1)], inc=VE)
    v2 = VE.n
    E("scalar", lambda e: e.activation(gate2[:], grow[:], AF.Sigmoid), waits=[(VE, v2)], inc=SE)
    se_g = SE.n
    E("vector", lambda e: e.tensor_copy(gate4[0:1, 0:2], gate2[:]), waits=[(SE, se_g)], inc=VE)
    E("vector", lambda e: e.tensor_scalar(gate4[0:1, 2:3], gate2[0:1, 0:1], -1.0, 1.0,
                                          op0=ALU.mult, op1=ALU.add), waits=[(VE, VE.n)], inc=VE)
    E("vector", lambda e: e.tensor_scalar(gate4[0:1, 3:4], gate2[0:1, 1:2], -1.0, None,
                                          op0=ALU.mult), waits=[(VE, VE.n)], inc=VE)
    ve_gate4 = VE.n
    s1 = E("scalar", lambda e: e.dma_start(out=gate_bc.ap(), in_=gate4[0:1, 2:4]),
      waits=[(VE, ve_gate4)], inc=SC)
    sc_gal = E("scalar", lambda e: e.dma_start(out=gal[:], in_=gate_bc.ap().broadcast_to([128, 2])),
      waits=[s1], inc=SC)
    if DBG:
        E("scalar", lambda e: e.dma_start(out=dbg["dbg_gates"], in_=gate4[:]),
          waits=[(VE, ve_gate4)], inc=SC)

    rot = {"i": 0}

    def next_bank():
        b = pm[rot["i"] % 4]
        rot["i"] += 1
        return b

    def mm_fm_set(wreg, src_si, evac, extra_w=(), nk=NDC, rhs_fn=None):
        for oc in range(NDC):
            for ts4 in range(4):
                bank = next_bank()
                w = bwait(bank)
                for kc in range(nk):
                    rhs = rhs_fn(kc, ts4) if rhs_fn else fm(src_si, kc, ts4 * 512, 512)
                    E("tensor", lambda e, b=bank, kc=kc, oc=oc, r=rhs: e.matmul(
                        b[:], wreg[:, kc * 1024 + oc * 128: kc * 1024 + oc * 128 + 128],
                        r, start=(kc == 0), stop=(kc == nk - 1)),
                      waits=(w if kc == 0 else []) + (list(extra_w) if (kc == 0) else []),
                      inc=(MM if kc == nk - 1 else None))
                free = evac(oc, ts4, bank, MM.n)
                bset(bank, *free)
                pump(2.5)

    def search_gen():
        g_prefetch(GA, waits=[sc_a1])
        g_prefetch(GB, waits=[sc_a2])
        for G in (GA, GB):
            G["sc_thrt"] = g_bcastx(G, "thrd_prev", tr8, tr_d, thrt, ve_mid0)
            G["sc_thtb"] = g_bcastx(G, "tbd_prev", tb8, tb_d, thtb, ve_mid0)
        for r in range(R_ROUNDS):
            ai = g_round(GA)
            bi = g_round(GB)
            next(ai)
            yield
            next(bi)
            yield
            for _ in ai:
                pass
            yield
            for _ in bi:
                pass
            yield
        sstate["thrd_prev"] = []
        sstate["pre_toks"] = [GA["pre"], GB["pre"]]
        yield from emit_search_extract()

    pump_state["gen"] = search_gen()

    def evac_to_slot(si, func=None):
        def f(oc, ts4, bank, mmv):
            if func is None:
                E("scalar", lambda e, b=bank, oc=oc, ts4=ts4: e.copy(
                    fm(si, oc, ts4 * 512, 512), b[:]), waits=[(MM, mmv)], inc=SE)
            else:
                E("scalar", lambda e, b=bank, oc=oc, ts4=ts4: e.activation(
                    fm(si, oc, ts4 * 512, 512), b[:], func), waits=[(MM, mmv)], inc=SE)
            return (SE, SE.n)
        return f

    mm_fm_set(RW0, 0, evac_to_slot(1), extra_w=[(SE, se_wk)] + sc_ln0fm)
    mm_k_done = MM.n
    se_k_done = SE.n
    mm_fm_set(RW1, 0, evac_to_slot(2), extra_w=[(SE, se_wv)])
    mm_v_done = MM.n
    if DBG:
        E("scalar", lambda e: e.dma_start(out=dbg["dbg_k"], in_=slots[1][:]),
          waits=[(SE, se_k_done)], inc=SC)

    se_wq = cast_weight(wqT, RW0, pre_waits=[(MM, mm_k_done)])

    qst = [slots_f[4][:, 0:256].bitcast(BF16), slots_f[4][:, 256:512].bitcast(BF16)]
    qkey = {}

    def evac_q(oc, ts4, bank, mmv):
        i = (oc * 4 + ts4) % 2
        st = qst[i]
        wq = qkey.get(str(i))
        E("scalar", lambda e, b=bank, s=st: e.copy(s, b[:]),
          waits=[(MM, mmv)] + ([wq] if wq else []), inc=SE)
        sev = SE.n
        qkey[str(i)] = E("scalar", lambda e, s=st, oc=oc, ts4=ts4: e.dma_start(
            out=q_dram.ap()[oc * 128:(oc + 1) * 128, ts4 * 512:(ts4 + 1) * 512], in_=s),
          waits=[(SE, sev)], inc=SC)
        return (SE, sev)

    mm_fm_set(RW0, 0, evac_q, extra_w=[(SE, se_wq)])
    mm_q_done = MM.n
    sc_q_done = SC.all()

    se_w0 = cast_weight(w0T, RW1, pre_waits=[(MM, mm_v_done)])
    se_w1T = cast_weight(w1T, RW0, pre_waits=[(MM, mm_q_done)])

    mark('B')
    d2tmp = slots_f[4][:, 2048:2560]
    ve_d2tmp = 0
    for sl in range(4):
        for ic in range(NDC):
            bank = next_bank()
            w = bwait(bank)
            for jc in range(NDC):
                E("tensor", lambda e, b=bank, jc=jc, ic=ic, sl=sl: e.matmul(
                    b[:], RW1[:, jc * 1024 + ic * 128: jc * 1024 + ic * 128 + 128],
                    fm(1, jc, sl * 512, 512), start=(jc == 0), stop=(jc == 7)),
                  waits=(w if jc == 0 else []) + (([(SE, se_w0)] + sc_q_done) if jc == 0 else []),
                  inc=(MM if jc == 7 else None))
            mmv = MM.n
            E("scalar", lambda e, b=bank, ic=ic: e.activation(
                slots[4][:, ic * 512:(ic + 1) * 512], b[:], AF.Silu),
              waits=[(MM, mmv)], inc=SE)
            bset(bank, SE, SE.n)
            pump(2.5)
        se_a1s = SE.n
        for dc in range(NDC):
            bank = next_bank()
            w = bwait(bank)
            for ec in range(NDC):
                E("tensor", lambda e, b=bank, ec=ec, dc=dc: e.matmul(
                    b[:], RW0[:, ec * 1024 + dc * 128: ec * 1024 + dc * 128 + 128],
                    slots[4][:, ec * 512:(ec + 1) * 512],
                    start=(ec == 0), stop=(ec == 7)),
                  waits=(w if ec == 0 else []) + ([(SE, se_w1T), (SE, se_a1s)] if ec == 0 else []),
                  inc=(MM if ec == 7 else None))
            mmv = MM.n
            E("vector", lambda e, b=bank, dc=dc, sl=sl: e.tensor_sub(
                d2tmp, b[:], fm(2, dc, sl * 512, 512)),
              waits=[(MM, mmv), (VE, ve_d2tmp)], inc=VE)
            v1 = VE.n
            bset(bank, VE, v1)
            E("vector", lambda e: e.tensor_scalar(
                d2tmp, d2tmp, 5.0, -5.0, op0=ALU.min, op1=ALU.max),
              waits=[(VE, v1)], inc=VE)
            v2 = VE.n
            E("vector", lambda e, dc=dc, sl=sl: e.tensor_scalar(
                fm(0, dc, sl * 512, 512), d2tmp, SCALE_D2, None, op0=ALU.mult),
              waits=[(VE, v2)], inc=VE)
            ve_d2tmp = VE.n
            pump(2.5)
    mm_pred_done = MM.n
    ve_d2_done = VE.n
    if DBG:
        E("scalar", lambda e: e.dma_start(out=dbg["dbg_d2"], in_=slots[0][:]),
          waits=[(VE, ve_d2_done)], inc=SC)

    mark('Cp')
    stmp = slots_f[2][:, 0:512]
    utmp = slots_f[2][:, 512:1024]
    wtmp = slots_f[2][:, 1024:1536]
    ve_c = 0
    se_c = 0
    for tc in range(NTC):
        for ih in range(2):
            bank = next_bank()
            w = bwait(bank)
            for jc in range(NDC):
                E("tensor", lambda e, b=bank, jc=jc, tc=tc, ih=ih: e.matmul(
                    b[:], fm(1, jc, tc * 128, 128),
                    RW1[:, jc * 1024 + ih * 512: jc * 1024 + ih * 512 + 512],
                    start=(jc == 0), stop=(jc == 7)),
                  waits=(w if jc == 0 else []) + ([(VE, ve_d2_done)] if (tc == 0 and ih == 0 and jc == 0) else []),
                  inc=(MM if jc == 7 else None))
            mmv = MM.n
            E("scalar", lambda e, b=bank, tc=tc, ih=ih: e.activation(
                tm(3, tc, ih * 512, 512), b[:], AF.Silu), waits=[(MM, mmv)], inc=SE)
            E("scalar", lambda e, b=bank: e.activation(stmp, b[:], AF.Sigmoid),
              waits=[(VE, ve_c)], inc=SE)
            se1_ = SE.n
            E("vector", lambda e: e.tensor_scalar(utmp, stmp, -1.0, 1.0,
                                                  op0=ALU.mult, op1=ALU.add),
              waits=[(SE, se1_), (VE, ve_c)], inc=VE)
            v1 = VE.n
            E("vector", lambda e, b=bank: e.tensor_mul(wtmp, b[:], utmp),
              waits=[(VE, v1)], inc=VE)
            v2 = VE.n
            bset(bank, VE, v2)
            E("vector", lambda e, tc=tc, ih=ih: e.scalar_tensor_tensor(
                tm(4, tc, ih * 512, 512), wtmp, 1.0, stmp, op0=ALU.add, op1=ALU.mult),
              waits=[(VE, v2)], inc=VE)
            ve_c = VE.n
            se_c = SE.n
            pump(2.5)
    mm_c_done = MM.n
    ve_c_done = VE.n

    se_w1s = cast_weight(w1s, RW0, pre_waits=[(MM, mm_c_done)], stage=cast_stage2)

    mark('C')
    def transpose_to_tm(src_si, dst_si, extra=()):
        for tc in range(NTC):
            bank = pt[tc % 2]
            w = bwait(bank)
            for dc in range(NDC):
                E("tensor", lambda e, b=bank, dc=dc, tc=tc: e.matmul(
                    b[:, dc * 128:(dc + 1) * 128], fm(src_si, dc, tc * 128, 128),
                    idb[:], is_transpose=True),
                  waits=(w if dc == 0 else []) + (list(extra) if (tc == 0 and dc == 0) else []),
                  inc=(MM if dc == 7 else None))
            mmv = MM.n
            E("scalar", lambda e, b=bank, tc=tc: e.copy(tm(dst_si, tc, 0, D), b[:]),
              waits=[(MM, mmv)], inc=SE)
            bset(bank, SE, SE.n)
            pump(1.5)
        return SE.n

    se_d2tm = transpose_to_tm(0, 2, extra=[(VE, ve_c_done), (SE, se_w1s)])

    mark('E')
    gst = [wa_f[:, 4096 + i * 512: 4096 + (i + 1) * 512] for i in range(2)]
    gkey = {}

    def g_mm(lhs_si, rhs_si, dst_dram, extra=()):
        for oc in range(NDC):
            for dh in range(2):
                bank = next_bank()
                w = bwait(bank)
                for tc in range(NTC):
                    E("tensor", lambda e, b=bank, tc=tc, oc=oc, dh=dh: e.matmul(
                        b[:], tm(lhs_si, tc, oc * 128, 128),
                        tm(rhs_si, tc, dh * 512, 512),
                        start=(tc == 0), stop=(tc == 15)),
                      waits=(w if tc == 0 else []) + (list(extra) if (oc == 0 and dh == 0 and tc == 0) else []),
                      inc=(MM if tc == 15 else None))
                mmv = MM.n
                i = (oc * 2 + dh) % 2
                st = gst[i]
                wg_ = gkey.get(str(i))
                E("vector", lambda e, b=bank, s=st: e.tensor_copy(s, b[:]),
                  waits=[(MM, mmv)] + ([wg_] if wg_ else []), inc=VE)
                vv = VE.n
                bset(bank, VE, vv)
                gkey[str(i)] = E("scalar", lambda e, s=st, oc=oc, dh=dh, d=dst_dram: e.dma_start(
                    out=d.ap()[oc * 128:(oc + 1) * 128, dh * 512:(dh + 1) * 512], in_=s),
                  waits=[(VE, vv)], inc=SC)
                pump(4.0)
        return SC.all()

    sc_g1 = g_mm(3, 2, g1_part, extra=[(SE, se_d2tm), (VE, ve_c_done)])

    mark('F')
    for tc in range(NTC):
        for eh in range(2):
            bank = next_bank()
            w = bwait(bank)
            for dc in range(NDC):
                E("tensor", lambda e, b=bank, dc=dc, tc=tc, eh=eh: e.matmul(
                    b[:], fm(0, dc, tc * 128, 128),
                    RW0[:, dc * 1024 + eh * 512: dc * 1024 + eh * 512 + 512],
                    start=(dc == 0), stop=(dc == 7)),
                  waits=(w if dc == 0 else []) + ([(SE, se_w1s)] if (tc == 0 and eh == 0 and dc == 0) else []),
                  inc=(MM if dc == 7 else None))
            mmv = MM.n
            E("vector", lambda e, b=bank, tc=tc, eh=eh: e.tensor_mul(
                tm(3, tc, eh * 512, 512), b[:], tm(4, tc, eh * 512, 512)),
              waits=[(MM, mmv)], inc=VE)
            bset(bank, VE, VE.n)
            pump(2.5)
    mm_d1_done = MM.n
    ve_d1_done = VE.n

    mark('G')
    se_ktm = transpose_to_tm(1, 0, extra=[(MM, mm_d1_done)])

    mark('H')
    sc_g0 = g_mm(0, 3, g0_part, extra=[(SE, se_ktm), (VE, ve_d1_done)])

    mark('I')
    E("gpsimd", lambda e: e.collective_compute(
        "AllReduce", ALU.add, replica_groups=[list(range(NCORES))],
        ins=[g1_part.ap().opt()], outs=[g1_sum.ap().opt()]),
      waits=sc_g1, inc=CC)
    cc_g1 = CC.n
    E("gpsimd", lambda e: e.collective_compute(
        "AllReduce", ALU.add, replica_groups=[list(range(NCORES))],
        ins=[g0_part.ap().opt()], outs=[g0_sum.ap().opt()]),
      waits=sc_g0, inc=CC)
    cc_g0 = CC.n

    mark('colls')
    gchunk = slots_f[1][:, 4096:5120]
    wchunk = slots_f[1][:, 5120:6144]
    ve_j = 0

    def w_update(gsum, wT_in, dst_bf, cc_wait):
        nonlocal_ve = [VE.n]
        for c in range(NDC):
            s1_ = E("sync", lambda e, c=c, g=gsum: e.dma_start(
                out=gchunk, in_=g.ap()[c * 128:(c + 1) * 128, :]),
              waits=[(CC, cc_wait), (VE, nonlocal_ve[0])], inc=SD)
            s2_ = E("sync", lambda e, c=c, w=wT_in: e.dma_start(
                out=wchunk, in_=w[c * 128:(c + 1) * 128, :]), inc=SD)
            E("vector", lambda e: e.tensor_scalar(
                gchunk, gchunk, 1.0, -1.0, op0=ALU.min, op1=ALU.max),
              waits=[s1_, (VE, nonlocal_ve[0])], inc=VE)
            v1_ = VE.n
            E("vector", lambda e: e.tensor_scalar(
                gchunk, gchunk, gal[:, 1:2], None, op0=ALU.mult),
              waits=[(VE, v1_), sc_gal], inc=VE)
            v2_ = VE.n
            E("vector", lambda e, c=c, d=dst_bf: e.scalar_tensor_tensor(
                d[:, c * 1024:(c + 1) * 1024], wchunk, gal[:, 0:1], gchunk,
                op0=ALU.mult, op1=ALU.add),
              waits=[(VE, v2_), s2_], inc=VE)
            nonlocal_ve[0] = VE.n
            pump(3.5)
        return VE.n

    W1nT_bf = slots[4][:, 0:8192]
    ve_w0n = w_update(g0_sum, w0T, RW1, cc_g0)
    ve_w1n = w_update(g1_sum, w1T, W1nT_bf, cc_g1)
    if DBG:
        E("scalar", lambda e: e.dma_start(out=dbg["dbg_w0nT"], in_=RW1),
          waits=[(VE, ve_w0n)], inc=SC)

    mark('J')
    sd_q = E("sync", lambda e: e.dma_start(
        out=slots[1][:].rearrange("p (c t) -> p c t", c=8),
        in_=q_dram.ap().rearrange("(c p) t -> p c t", p=128)),
      waits=sc_q_done + [(VE, ve_w1n)], inc=SD)

    mm_fm_set(RW1, 1, evac_to_slot(2, AF.Silu), extra_w=[sd_q, (VE, ve_w0n)])
    mm_h1_done = MM.n
    se_h1 = SE.n

    mm_fm_set(W1nT_bf, 2, evac_to_slot(3), extra_w=[(SE, se_h1), (VE, ve_w1n)])
    mm_h_done = MM.n
    cast_stage0 = [slots_f[0][:, 0:1024], slots_f[0][:, 1024:2048]]
    se_wout = cast_weight(woutT, RW1, pre_waits=[(MM, mm_h1_done)] + SC.all(),
                          stage=cast_stage0)
    mm_fm_set(RW1, 3, evac_to_slot(1), extra_w=[(SE, se_wout)])
    mm_att_done = MM.n
    se_att_done = SE.n
    if DBG:
        E("scalar", lambda e: e.dma_start(out=dbg["dbg_attn"], in_=slots[1][:]),
          waits=[(SE, se_att_done)], inc=SC)

    mark('K')
    se_atm = transpose_to_tm(1, 2, extra=[(SE, se_att_done)])

    sq2 = slots_f[1][:, 0:1024]
    ve_sq = 0
    se_sq = 0
    sc_dbgL = SC.all()
    for tc in range(NTC):
        E("vector", lambda e, tc=tc: e.reduce_sum(stat[:, tc:tc + 1], tm(2, tc, 0, D), axis=AX.X),
          waits=[(SE, se_atm)], inc=VE)
        E("scalar", lambda e, tc=tc: e.activation(sq2, tm(2, tc, 0, D), AF.Square),
          waits=[(SE, se_atm), (VE, ve_sq)] + sc_dbgL, inc=SE)
        sec = SE.n
        E("vector", lambda e, tc=tc: e.reduce_sum(stat2[:, tc:tc + 1], sq2, axis=AX.X),
          waits=[(SE, sec)], inc=VE)
        ve_sq = VE.n
        pump(3.0)
    ve_la = stats_chain(VE.n, stat, stat2, mcol, rcol)

    xb2 = [slots_f[1][:, 1024:2048], slots_f[1][:, 2048:3072]]
    y1b = [slots_f[1][:, 3072:4096], slots_f[1][:, 4096:5120]]
    lkey = {"x0": 0, "x1": 0}
    for tc in range(NTC):
        bx = xb2[tc % 2]
        by = y1b[tc % 2]
        sdc = E("sync", lambda e, tc=tc, b=bx: e.dma_start(
            out=b, in_=x_tm[tc * 128:(tc + 1) * 128, :]),
          waits=[(VE, lkey[f"x{tc % 2}"]), (VE, ve_la)], inc=SD)
        wy = lkey.get(f"y{tc % 2}")
        E("vector", lambda e, tc=tc, b=by: e.tensor_scalar(
            b, tm(2, tc, 0, D), mcol[:, tc:tc + 1], rcol[:, tc:tc + 1],
            op0=ALU.subtract, op1=ALU.mult),
          waits=([wy] if wy else []) + [(VE, VE.n)], inc=VE)
        v1 = VE.n
        E("vector", lambda e, bx=bx, by=by: e.tensor_add(by, by, bx),
          waits=[(VE, v1), sdc], inc=VE)
        v2 = VE.n
        E("vector", lambda e, tc=tc, by=by: e.reduce_sum(stat[:, tc:tc + 1], by, axis=AX.X),
          waits=[(VE, v2)], inc=VE)
        lkey[f"x{tc % 2}"] = VE.n
        E("scalar", lambda e, by=by: e.activation(sq2, by, AF.Square),
          waits=[(VE, v2), (VE, ve_sq)], inc=SE)
        sec = SE.n
        lkey[f"se{tc % 2}"] = sec
        E("vector", lambda e, tc=tc: e.reduce_sum(stat2[:, tc:tc + 1], sq2, axis=AX.X),
          waits=[(SE, sec)], inc=VE)
        ve_sq = VE.n
        lkey[f"y{tc % 2}"] = E("scalar", lambda e, tc=tc, by=by: e.dma_start(
            out=y1_dram.ap()[tc * 128:(tc + 1) * 128, :], in_=by),
          waits=[(VE, v2)], inc=SC)
        pump(5.0)
    sc_y1 = SC.all()
    ve_ly = stats_chain(VE.n, stat, stat2, mcol, rcol)

    ub = [slots[1][:, 10240:11264], slots[1][:, 11264:12288]]
    for tc in range(NTC):
        by = y1b[tc % 2]
        sdc = E("sync", lambda e, tc=tc, b=by: e.dma_start(
            out=b, in_=y1_dram.ap()[tc * 128:(tc + 1) * 128, :]),
          waits=sc_y1 + [(VE, lkey[f"x{tc % 2}"]),
                         (SE, lkey.get(f"se{tc % 2}", 0))], inc=SD)
        bu = ub[tc % 2]
        E("vector", lambda e, tc=tc, b=by, u=bu: e.tensor_scalar(
            u, b, mcol[:, tc:tc + 1], rcol[:, tc:tc + 1], op0=ALU.subtract, op1=ALU.mult),
          waits=[sdc, (VE, ve_ly), lkey[f"y{tc % 2}"]], inc=VE)
        v1 = VE.n
        lkey[f"x{tc % 2}"] = v1
        E("vector", lambda e, tc=tc, u=bu: e.reduce_sum(stat[:, tc:tc + 1], u, axis=AX.X),
          waits=[(VE, v1)], inc=VE)
        E("scalar", lambda e, u=bu: e.activation(sq2, u, AF.Square),
          waits=[(VE, v1), (VE, ve_sq)], inc=SE)
        sec = SE.n
        E("vector", lambda e, tc=tc: e.reduce_sum(stat2[:, tc:tc + 1], sq2, axis=AX.X),
          waits=[(SE, sec)], inc=VE)
        ve_sq = VE.n
        lkey[f"y{tc % 2}"] = E("scalar", lambda e, tc=tc, u=bu: e.dma_start(
            out=ublk.ap()[:, tc * 128:(tc + 1) * 128, :].rearrange("g p d -> p g d"),
            in_=u.rearrange("p (g d) -> p g d", g=8)),
          waits=[(VE, v1)], inc=SC)
        pump(5.0)
    sc_ublk = SC.all()
    ve_lu = stats_chain(VE.n, stat, stat2, muU, rU)
    E("vector", lambda e: e.tensor_mul(rmU[:], rU[:], muU[:]), waits=[(VE, ve_lu)], inc=VE)
    ve_rmU = VE.n

    mark('L')
    for dc in range(NDC):
        E("scalar", lambda e, dc=dc: e.dma_start(
            out=fm(1, dc, 0, TOK), in_=ublk.ap()[dc], transpose=True),
          waits=sc_ublk, inc=SC)
    sc_ufm = SC.all()

    t_wr = E("sync", lambda e: e.dma_start(
        out=wr_sb[:].rearrange("p (c e) -> p c e", c=8),
        in_=wr_i.rearrange("(c p) e -> p c e", p=128)), inc=SD)
    t_gam = E("sync", lambda e: e.dma_start(out=gam_sb[:], in_=gamg_i), inc=SD)
    t_bet = E("sync", lambda e: e.dma_start(out=bet_sb[:], in_=betg_i), inc=SD)
    t_br = E("sync", lambda e: e.dma_start(out=br_sb[:], in_=br_i), inc=SD)
    sd_wr = [t_wr, t_gam, t_bet, t_br]
    E("vector", lambda e: e.memset(onesb[:], 1.0), inc=VE)
    ve_ones = VE.n
    for c in range(NDC):
        E("vector", lambda e, c=c: e.tensor_scalar(
            wrp[:, c * 8:(c + 1) * 8], wr_sb[:, c * 8:(c + 1) * 8],
            gam_sb[:, c:c + 1], None, op0=ALU.mult),
          waits=sd_wr, inc=VE)
        E("vector", lambda e, c=c: e.tensor_scalar(
            wrb[:, c * 8:(c + 1) * 8], wr_sb[:, c * 8:(c + 1) * 8],
            bet_sb[:, c:c + 1], None, op0=ALU.mult), inc=VE)
    ve_wrp = VE.n
    w = bwait(pr)
    for c in range(NDC):
        E("tensor", lambda e, c=c: e.matmul(
            pr[0:1, 0:8], onesb[:], wrp[:, c * 8:(c + 1) * 8],
            start=(c == 0), stop=(c == 7)),
          waits=(w if c == 0 else []) + ([(VE, ve_wrp), (VE, ve_ones)] if c == 0 else []),
          inc=(MM if c == 7 else None))
    mm1 = MM.n
    E("vector", lambda e: e.tensor_copy(c1r[:], pr[0:1, 0:8]), waits=[(MM, mm1)], inc=VE)
    ve_c1 = VE.n
    bset(pr, VE, ve_c1)
    w = bwait(pr)
    for c in range(NDC):
        E("tensor", lambda e, c=c: e.matmul(
            pr[0:1, 0:8], onesb[:], wrb[:, c * 8:(c + 1) * 8],
            start=(c == 0), stop=(c == 7)),
          waits=(w if c == 0 else []), inc=(MM if c == 7 else None))
    mm2 = MM.n
    E("vector", lambda e: e.tensor_copy(c2r[:], pr[0:1, 0:8]), waits=[(MM, mm2)], inc=VE)
    v1 = VE.n
    bset(pr, VE, v1)
    E("vector", lambda e: e.tensor_add(c2r[:], c2r[:], br_sb[:]), waits=[(VE, v1), t_br], inc=VE)
    ve_c2 = VE.n
    tc1 = E("scalar", lambda e: e.dma_start(out=cbc_d.ap()[0:1, 0:8], in_=c1r[:]),
      waits=[(VE, ve_c1)], inc=SC)
    tc2 = E("scalar", lambda e: e.dma_start(out=cbc_d.ap()[0:1, 8:16], in_=c2r[:]),
      waits=[(VE, ve_c2)], inc=SC)
    sc_c12 = E("scalar", lambda e: e.dma_start(out=c12bc[:], in_=cbc_d.ap().broadcast_to([128, 16])),
      waits=[tc1, tc2], inc=SC)

    for tc in range(NTC):
        w = bwait(pr)
        for hc in range(NDC):
            E("tensor", lambda e, hc=hc, tc=tc: e.matmul(
                pr[:, 0:8], fm(1, hc, tc * 128, 128), wrp[:, hc * 8:(hc + 1) * 8],
                start=(hc == 0), stop=(hc == 7)),
              waits=(w if hc == 0 else []) + (sc_ufm if (tc == 0 and hc == 0) else []),
              inc=(MM if hc == 7 else None))
        mmv = MM.n
        E("vector", lambda e, tc=tc: e.tensor_scalar(
            lgt[:], pr[:, 0:8], rU[:, tc:tc + 1], None, op0=ALU.mult),
          waits=[(MM, mmv), (VE, ve_rmU), sc_c12], inc=VE)
        v1 = VE.n
        bset(pr, VE, v1)
        E("vector", lambda e, tc=tc: e.scalar_tensor_tensor(
            lg2[:], c12bc[:, 0:8], rmU[:, tc:tc + 1], lgt[:],
            op0=ALU.mult, op1=ALU.subtract),
          waits=[(VE, v1)], inc=VE)
        v2 = VE.n
        E("vector", lambda e: e.tensor_sub(lg3[:], c12bc[:, 8:16], lg2[:]),
          waits=[(VE, v2)], inc=VE)
        v3 = VE.n
        E("vector", lambda e: e.reduce_max(sm[0][:], lg3[:], axis=AX.X),
          waits=[(VE, v3)], inc=VE)
        v4 = VE.n
        E("vector", lambda e: e.tensor_scalar(sm[1][:], sm[0][:], -1.0, None, op0=ALU.mult),
          waits=[(VE, v4)], inc=VE)
        v5 = VE.n
        E("scalar", lambda e: e.activation(expv[:], lg3[:], AF.Exp, bias=sm[1][:]),
          waits=[(VE, v5)], inc=SE)
        s1_ = SE.n
        E("vector", lambda e: e.reduce_sum(sm[2][:], expv[:], axis=AX.X),
          waits=[(SE, s1_)], inc=VE)
        v6 = VE.n
        E("vector", lambda e: e.reciprocal(sm[3][:], sm[2][:]), waits=[(VE, v6)], inc=VE)
        v7 = VE.n
        E("vector", lambda e, tc=tc: e.tensor_mul(pcc[:, tc:tc + 1], expv[:, 0:1], sm[3][:]),
          waits=[(VE, v7)], inc=VE)
        E("vector", lambda e: e.tensor_sub(sm[4][:], sm[2][:], expv[:, 0:1]),
          waits=[(VE, v7)], inc=VE)
        v8 = VE.n
        E("vector", lambda e, tc=tc: e.tensor_mul(ppc[:, tc:tc + 1], sm[4][:], sm[3][:]),
          waits=[(VE, v8)], inc=VE)
        pump(2.0)
    ve_pc = VE.n
    tp1 = E("scalar", lambda e: e.dma_start(out=pcc_d.ap(), in_=pcc[:]),
      waits=[(VE, ve_pc)], inc=SC)
    tp2 = E("scalar", lambda e: e.dma_start(out=ppc_d.ap(), in_=ppc[:]),
      inc=SC)
    mark('M')
    cst_f = [slots_f[3][:, 0:4096], slots_f[3][:, 4096:8192]]
    cst_b = [wa_b[:, 8192:12288], wa_b[:, 12288:16384]]
    ckey = {"0": (0, None), "1": (0, None)}

    def cast_chunk(i, n, src_ap, dst_ap, view=None):
        sf = cst_f[i % 2][:, 0:n]
        sb = cst_b[i % 2][:, 0:n]
        se_prev, sc_prev = ckey[str(i % 2)]
        sdc = E("sync", lambda e, s=sf, src=src_ap: e.dma_start(out=s, in_=src),
          waits=[(SE, se_prev), (MM, mm_att_done)], inc=SD)
        E("scalar", lambda e, o=sb, s=sf: e.copy(o, s),
          waits=[sdc, (MM, mm_att_done)] + ([sc_prev] if sc_prev else []), inc=SE)
        sev = SE.n
        tok = E("scalar", lambda e, o=dst_ap, s=sb, v=view: e.dma_start(
            out=o, in_=(v(s) if v else s)),
          waits=[(SE, sev)], inc=SC)
        ckey[str(i % 2)] = (sev, tok)
        pump(1.5)

    for hc in range(NDC):
        cast_chunk(hc, 4096,
                   w1ff[hc * 128:(hc + 1) * 128, :],
                   w1t_loc.ap().rearrange("(f h) p a -> h p f a", h=8)[hc],
                   view=lambda t: t.rearrange("p (f a) -> p f a", a=128))
    sc_w1cast = SC.all()
    for r in range(32):
        cast_chunk(r, 1024,
                   w2ff[r * 128:(r + 1) * 128, :],
                   w2bf_loc.ap()[r * 128:(r + 1) * 128, :])
    sc_w2cast = SC.all()

    mark('cast')
    wkey = {}

    def slab(name, i, off):
        return wa_b[:, off + (i % 2) * 1024: off + (i % 2) * 1024 + 1024]

    for sl in range(4):
        for ft in range(32):
            w1sb = slab("w1", ft, 0)
            ts1 = E("sync", lambda e, ft=ft, b=w1sb: e.dma_start(
                out=b.rearrange("p (h f) -> p h f", h=8),
                in_=w1t_loc.ap().rearrange("(f h) p a -> f p h a", h=8)[ft]),
              waits=[(MM, wkey.get(f"w1{ft % 2}", 0))] + (sc_w1cast if sl == 0 else []),
              inc=SD)
            bankA = next_bank()
            wA = bwait(bankA)
            for hc in range(NDC):
                E("tensor", lambda e, b=bankA, hc=hc, sl=sl, s=w1sb: e.matmul(
                    b[:], s[:, hc * 128:(hc + 1) * 128], fm(1, hc, sl * 512, 512),
                    start=(hc == 0), stop=(hc == 7)),
                  waits=((wA + [ts1, (MM, mm_att_done)]) if hc == 0 else []),
                  inc=(MM if hc == 7 else None))
            mm_pre = MM.n
            wkey[f"w1{ft % 2}"] = mm_pre
            E("scalar", lambda e, b=bankA, ft=ft: e.activation(
                slots[2][:, ft * 512:(ft + 1) * 512], b[:], AF.Gelu_apprx_tanh,
                bias=b1gs[:, ft:ft + 1]),
              waits=[(MM, mm_pre)], inc=SE)
            bset(bankA, SE, SE.n)
            pump(3.0)
        se_acts = SE.n
        mark(f"Oc{sl}a")
        for half in range(2):
            banks = [pm[i] for i in range(4)]
            bw = [bwait(bk) for bk in banks]
            for fc in range(32):
                w2sb = slab("w2", fc, 4096)
                sd_w2 = E("sync", lambda e, fc=fc, b=w2sb: e.dma_start(
                    out=b, in_=w2bf_loc.ap()[fc * 128:(fc + 1) * 128, :]),
                  waits=[(MM, wkey.get(f"w2{fc % 2}", 0))]
                        + (sc_w2cast if (sl == 0 and half == 0) else []),
                  inc=SD)
                for h4 in range(4):
                    hcg = half * 4 + h4
                    E("tensor", lambda e, b=banks[h4], fc=fc, hcg=hcg, s=w2sb, sl=sl: e.matmul(
                        b[:], s[:, hcg * 128:(hcg + 1) * 128],
                        slots[2][:, fc * 512:(fc + 1) * 512],
                        start=(fc == 0), stop=(fc == 31)),
                      waits=(([sd_w2, (SE, se_acts)] + bw[h4]) if fc == 0 else ([sd_w2] if h4 == 0 else [])),
                      inc=(MM if h4 == 3 else None))
                wkey[f"w2{fc % 2}"] = MM.n
            mm_y = MM.n
            for h4 in range(4):
                hcg = half * 4 + h4
                E("vector", lambda e, b=banks[h4], hcg=hcg, sl=sl: e.tensor_scalar(
                    slots[4][:, sl * 4096 + hcg * 512: sl * 4096 + hcg * 512 + 512],
                    b[:], b2gs[:, hcg:hcg + 1], None, op0=ALU.add),
                  waits=[(MM, mm_y)], inc=VE)
                bset(banks[h4], VE, VE.n)
        mark(f"Oc{sl}y")

    pump_flush()
    ve_srch = VE.n
    sc_thrf = sstate["sc_thrf"]

    tg1 = E("gpsimd", lambda e: e.dma_start(
        out=pcd1.ap().rearrange("a (c p) -> (a c) p", p=128),
        in_=pcc_d.ap().rearrange("p c -> c p"),
        allow_slow_non_contiguous=True),
      waits=[tp1, tp2], inc=GD)
    tg2 = E("gpsimd", lambda e: e.dma_start(
        out=pcd2.ap().rearrange("a (c p) -> (a c) p", p=128),
        in_=ppc_d.ap().rearrange("p c -> c p"),
        allow_slow_non_contiguous=True),
      inc=GD)
    Pc = sbig_b[:, 0:2048]
    Pp = sbig_b[:, 2048:4096]
    tP1 = E("scalar", lambda e: e.dma_start(out=Pc, in_=pcd1.ap().broadcast_to([128, TOK])),
      waits=[tg1, tg2, (VE, ve_srch)], inc=SC)
    tP2 = E("scalar", lambda e: e.dma_start(out=Pp, in_=pcd2.ap().broadcast_to([128, TOK])),
      inc=SC)
    sc_P = [tP1, tP2]
    if DBG:
        E("scalar", lambda e: e.dma_start(out=dbg["dbg_pc"][0:1, :], in_=pcd1.ap()),
          waits=[tg1, tg2], inc=SC)
        E("scalar", lambda e: e.dma_start(out=dbg["dbg_pc"][1:2, :], in_=pcd2.ap()),
          inc=SC)

    def pert_b(src, absrc, dst, w, thc_b, thc_t):
        E("vector", lambda e: e.tensor_scalar(
            tb1[:, 0:w], absrc[:, 0:w], thrf[:, thc_b:thc_b + 1], None, op0=ALU.is_le),
          waits=sc_thrf + [(VE, VE.n)], inc=VE)
        E("vector", lambda e: e.tensor_scalar(
            tb2[:, 0:w], absrc[:, 0:w], thrf[:, thc_t:thc_t + 1], None, op0=ALU.is_ge),
          waits=[(VE, VE.n)], inc=VE)
        E("vector", lambda e: e.tensor_sub(tb1[:, 0:w], tb1[:, 0:w], tb2[:, 0:w]),
          waits=[(VE, VE.n)], inc=VE)
        E("vector", lambda e: e.tensor_scalar(tb2[:, 0:w], src[:, 0:w], 0.8, None, op0=ALU.mult),
          waits=[(VE, VE.n)], inc=VE)
        E("vector", lambda e: e.tensor_mul(tb1[:, 0:w], tb2[:, 0:w], tb1[:, 0:w]),
          waits=[(VE, VE.n)], inc=VE)
        E("vector", lambda e: e.tensor_add(dst[:, 0:w], src[:, 0:w], tb1[:, 0:w]),
          waits=[(VE, VE.n)], inc=VE)
        return VE.n

    pert_b(b1gs, b1a, pb1g, 32, 4, 5)
    ve_pb1 = VE.n
    pert_b(b2gs, b2a, pb2g, 8, 6, 7)
    ve_pb2 = VE.n

    dg_f32 = slots_f[0][:, 0:4096]
    dg_A = slots_f[0][:, 4096:8192]
    dg_B = slots[2][:, 0:4096]

    def dgen(slice_in, thb, tht, store_out, store_view):
        sc_prev = SC.all()
        sdc = E("sync", lambda e: e.dma_start(out=dg_f32, in_=slice_in),
          waits=[(VE, VE.n), (SE, SE.n)] + sc_prev, inc=SD)
        E("vector", lambda e: e.tensor_scalar(
            dg_A, dg_f32, thrf[:, thb:thb + 1], None, op0=ALU.is_le),
          waits=[sdc, (MM, MM.n)] + sc_thrf + sc_prev, inc=VE)
        E("vector", lambda e: e.tensor_scalar(
            dg_B, dg_f32, nthrf[:, thb:thb + 1], None, op0=ALU.is_ge),
          waits=[(VE, VE.n), (MM, MM.n)], inc=VE)
        E("vector", lambda e: e.tensor_mul(dg_A, dg_A, dg_B), waits=[(VE, VE.n)], inc=VE)
        E("vector", lambda e: e.tensor_scalar(
            dg_B, dg_f32, thrf[:, tht:tht + 1], None, op0=ALU.is_ge),
          waits=[(VE, VE.n)], inc=VE)
        E("vector", lambda e: e.tensor_sub(dg_A, dg_A, dg_B), waits=[(VE, VE.n)], inc=VE)
        E("vector", lambda e: e.tensor_scalar(
            dg_B, dg_f32, nthrf[:, tht:tht + 1], None, op0=ALU.is_le),
          waits=[(VE, VE.n)], inc=VE)
        E("vector", lambda e: e.tensor_sub(dg_A, dg_A, dg_B), waits=[(VE, VE.n)], inc=VE)
        E("vector", lambda e: e.tensor_mul(dg_A, dg_A, dg_f32), waits=[(VE, VE.n)], inc=VE)
        E("vector", lambda e: e.tensor_scalar(dg_A, dg_A, 0.8, None, op0=ALU.mult),
          waits=[(VE, VE.n)], inc=VE)
        E("vector", lambda e: e.tensor_add(dg_B, dg_A, dg_f32), waits=[(VE, VE.n)], inc=VE)
        ve_p = VE.n
        t = E("scalar", lambda e: e.dma_start(out=store_out, in_=store_view(dg_B)),
          waits=[(VE, ve_p)], inc=SC)
        return [t]

    sc_dg1 = dgen(
        w1sl, 0, 1,
        w1p_in.ap().rearrange("f p a -> p f a"),
        lambda t: t.rearrange("p (f a) -> p f a", a=128))
    sc_dg2 = dgen(
        w2sl, 2, 3,
        pw2_in.ap().rearrange("(p a) d -> p a d", p=128),
        lambda t: t.rearrange("p (a d) -> p a d", d=D))
    ve_dgen = VE.n

    mark('dgen')
    E("gpsimd", lambda e: e.collective_compute(
        "AllGather", ALU.bypass, replica_groups=[list(range(NCORES))],
        ins=[w1p_in.ap().opt()], outs=[w1p_d.ap().opt()]),
      waits=sc_dg1, inc=CC)
    E("gpsimd", lambda e: e.collective_compute(
        "AllGather", ALU.bypass, replica_groups=[list(range(NCORES))],
        ins=[pw2_in.ap().opt()], outs=[pw2_d.ap().opt()]),
      waits=sc_dg2, inc=CC)
    cc_ag = CC.n

    mark('colls')
    y1t = slots_f[0][:, 0:4096]
    wkey["y1t"] = ve_dgen
    for sl in range(4):
        sd_y1t = E("sync", lambda e, sl=sl: e.dma_start(
            out=y1t.rearrange("p (c d) -> p c d", c=4),
            in_=y1_dram.ap()[sl * 512:(sl + 1) * 512, :].rearrange("(c p) d -> p c d", p=128)),
          waits=sc_y1 + [(VE, wkey.get("y1t", 0))], inc=SD)
        for ft in range(32):
            w1sb = slab("w1", ft, 0)
            ts1 = E("sync", lambda e, ft=ft, b=w1sb: e.dma_start(
                out=b.rearrange("p (h f) -> p h f", h=8),
                in_=w1p_d.ap().rearrange("(r t) p f -> r t p f", t=32)[:, ft].rearrange("r p f -> p r f")),
              waits=[(CC, cc_ag), (MM, wkey.get(f"w1{ft % 2}", 0))], inc=SD)
            bankA = next_bank()
            wA = bwait(bankA)
            for hc in range(NDC):
                E("tensor", lambda e, b=bankA, hc=hc, sl=sl, s=w1sb: e.matmul(
                    b[:], s[:, hc * 128:(hc + 1) * 128], fm(1, hc, sl * 512, 512),
                    start=(hc == 0), stop=(hc == 7)),
                  waits=((wA + [ts1]) if hc == 0 else []),
                  inc=(MM if hc == 7 else None))
            mm_pre = MM.n
            wkey[f"w1{ft % 2}"] = mm_pre
            E("scalar", lambda e, b=bankA, ft=ft: e.activation(
                slots[2][:, ft * 512:(ft + 1) * 512], b[:], AF.Gelu_apprx_tanh,
                bias=pb1g[:, ft:ft + 1]),
              waits=[(MM, mm_pre), (VE, ve_pb1)], inc=SE)
            bset(bankA, SE, SE.n)
        se_acts = SE.n
        mark(f"Op{sl}a")
        for half in range(2):
            banks = [pm[i] for i in range(4)]
            bw = [bwait(bk) for bk in banks]
            for fc in range(32):
                w2sb = slab("w2", fc, 4096)
                sd_w2 = E("sync", lambda e, fc=fc, b=w2sb: e.dma_start(
                    out=b, in_=pw2_d.ap()[fc * 128:(fc + 1) * 128, :]),
                  waits=[(CC, cc_ag), (MM, wkey.get(f"w2{fc % 2}", 0))], inc=SD)
                for h4 in range(4):
                    hcg = half * 4 + h4
                    E("tensor", lambda e, b=banks[h4], fc=fc, hcg=hcg, s=w2sb, sl=sl: e.matmul(
                        b[:], s[:, hcg * 128:(hcg + 1) * 128],
                        slots[2][:, fc * 512:(fc + 1) * 512],
                        start=(fc == 0), stop=(fc == 31)),
                      waits=(([sd_w2, (SE, se_acts)] + bw[h4]) if fc == 0 else ([sd_w2] if h4 == 0 else [])),
                      inc=(MM if h4 == 3 else None))
                wkey[f"w2{fc % 2}"] = MM.n
            mm_y = MM.n
            for h4 in range(4):
                hcg = half * 4 + h4
                E("vector", lambda e, b=banks[h4], hcg=hcg: e.tensor_scalar(
                    slots[0][:, 8192 + hcg * 512: 8192 + hcg * 512 + 512],
                    b[:], pb2g[:, hcg:hcg + 1], None, op0=ALU.add),
                  waits=[(MM, mm_y), (VE, ve_pb2)], inc=VE)
                bset(banks[h4], VE, VE.n)
        mark(f"Op{sl}y")
        ctmp = wa_f[:, 3072:3584]
        ve_comb = 0
        for hcg in range(NDC):
            E("vector", lambda e, hcg=hcg, sl=sl: e.tensor_tensor(
                ctmp, slots[4][:, sl * 4096 + hcg * 512: sl * 4096 + hcg * 512 + 512],
                Pc[:, sl * 512:(sl + 1) * 512], op=ALU.mult),
              waits=[(VE, VE.n), (VE, ve_comb)] + sc_P
                    + ([wkey["ob"]] if "ob" in wkey else []), inc=VE)
            v1 = VE.n
            E("vector", lambda e, hcg=hcg, sl=sl: e.tensor_tensor(
                wa_f[:, 3584:4096], slots[0][:, 8192 + hcg * 512: 8192 + hcg * 512 + 512],
                Pp[:, sl * 512:(sl + 1) * 512], op=ALU.mult),
              waits=[(VE, v1)], inc=VE)
            v2 = VE.n
            E("vector", lambda e, hcg=hcg, sl=sl: e.tensor_add(
                slots[4][:, sl * 4096 + hcg * 512: sl * 4096 + hcg * 512 + 512],
                ctmp, wa_f[:, 3584:4096]),
              waits=[(VE, v2)], inc=VE)
            ve_comb = VE.n
        ve_fo = VE.n
        mark(f"Op{sl}c")
        for tcl in range(4):
            bank = pt[tcl % 2]
            w = bwait(bank)
            for hcg in range(NDC):
                E("tensor", lambda e, b=bank, hcg=hcg, tcl=tcl, sl=sl: e.matmul(
                    b[:, hcg * 128:(hcg + 1) * 128],
                    slots[4][:, sl * 4096 + hcg * 512 + tcl * 128: sl * 4096 + hcg * 512 + tcl * 128 + 128],
                    idb[:], is_transpose=True),
                  waits=(w + [(VE, ve_fo)] if hcg == 0 else []),
                  inc=(MM if hcg == 7 else None))
            mmv = MM.n
            obsq = wa_f[:, 5120:6144]
            E("scalar", lambda e, b=bank, o=obsq: e.activation(o, b[:], AF.Square),
              waits=[(MM, mmv), (VE, ve_sq)], inc=SE)
            s1_ = SE.n
            E("vector", lambda e, b=bank: e.reduce_sum(sm[0][:], b[:], axis=AX.X),
              waits=[(MM, mmv), (SE, s1_)], inc=VE)
            E("vector", lambda e, o=obsq: e.reduce_sum(sm[1][:], o, axis=AX.X),
              waits=[(SE, s1_)], inc=VE)
            v1 = VE.n
            E("vector", lambda e: e.tensor_scalar(sm[2][:], sm[0][:], 1.0 / D, None, op0=ALU.mult),
              waits=[(VE, v1)], inc=VE)
            v2 = VE.n
            E("vector", lambda e: e.tensor_mul(sm[3][:], sm[2][:], sm[2][:]),
              waits=[(VE, v2)], inc=VE)
            v3 = VE.n
            E("vector", lambda e: e.scalar_tensor_tensor(
                sm[4][:], sm[1][:], 1.0 / D, sm[3][:], op0=ALU.mult, op1=ALU.subtract),
              waits=[(VE, v3)], inc=VE)
            v4 = VE.n
            E("scalar", lambda e: e.activation(sm[5][:], sm[4][:], AF.Sqrt, bias=epsb[:]),
              waits=[(VE, v4)], inc=SE)
            s2_ = SE.n
            E("vector", lambda e: e.reciprocal(sm[5][:], sm[5][:]), waits=[(SE, s2_)], inc=VE)
            v5 = VE.n
            ob = wa_f[:, 4096:5120]
            E("vector", lambda e, b=bank: e.tensor_scalar(
                ob, b[:], sm[2][:], sm[5][:], op0=ALU.subtract, op1=ALU.mult),
              waits=[(VE, v5), (VE, ve_comb)] + ([wkey["ob"]] if "ob" in wkey else []), inc=VE)
            v6 = VE.n
            E("vector", lambda e, tcl=tcl: e.tensor_add(
                ob, ob, y1t[:, tcl * 1024:(tcl + 1) * 1024]),
              waits=[(VE, v6), sd_y1t], inc=VE)
            v7 = VE.n
            ve_sq = v7
            wkey["y1t"] = v7
            bset(bank, VE, v7)
            wkey["ob"] = E("scalar", lambda e, sl=sl, tcl=tcl: e.dma_start(
                out=out_tm[sl * 512 + tcl * 128: sl * 512 + tcl * 128 + 128, :], in_=ob),
              waits=[(VE, v7)], inc=SC)

    E("sync", lambda e: e.nop(),
      waits=SC.all() + SD.all() + GD.all())

    mark("end")
    globals()["MARKS"] = dict(marks)
    _stop = stop_at if stop_at is not None else STOP_AT
    if _stop is not None:
        cut = marks[_stop]
        for k in plan:
            plan[k] = [f for f in plan[k] if f._seq < cut]

    with ctx:
        block = ctx.enter_context(nc.Block())

        @block.sync
        def _(sync):
            for fn in plan["sync"]:
                fn(sync)

        @block.scalar
        def _(s):
            for fn in plan["scalar"]:
                fn(s)

        @block.vector
        def _(v):
            for fn in plan["vector"]:
                fn(v)

        @block.tensor
        def _(t):
            for fn in plan["tensor"]:
                fn(t)

        @block.gpsimd
        def _(g):
            for fn in plan["gpsimd"]:
                fn(g)

    return nc


_NC_CACHE = {}


def _get_nc():
    key = STOP_AT
    if key not in _NC_CACHE:
        _NC_CACHE[key] = build_nc()
    return _NC_CACHE[key]


def kernel(**inputs):
    ii = inputs
    x = np.asarray(ii["L0_attn_x"], dtype=np.float32)

    def T(a):
        return np.ascontiguousarray(np.asarray(a, dtype=np.float32).T)

    wkT = T(ii["L0_attn_Wk"])
    wvT = T(ii["L0_attn_Wv"])
    wqT = T(ii["L0_attn_Wq"])
    woutT = T(ii["L0_attn_Wout"])
    w0T = T(ii["L0_attn_W0"])
    w1T = T(ii["L0_attn_W1"])
    w1s = np.ascontiguousarray(np.asarray(ii["L0_attn_W1"], dtype=np.float32))
    wg2 = np.ascontiguousarray(np.stack(
        [np.asarray(ii["L0_attn_Wg_decay"], np.float32)[0],
         np.asarray(ii["L0_attn_Wg_lr"], np.float32)[0]], axis=1))
    bg2 = np.array([[float(np.asarray(ii["L0_attn_bg_decay"]).ravel()[0]),
                     float(np.asarray(ii["L0_attn_bg_lr"]).ravel()[0])]], np.float32)
    w1f = np.ascontiguousarray(np.asarray(ii["L0_ffn_W1"], np.float32))
    w2f = np.ascontiguousarray(np.asarray(ii["L0_ffn_W2"], np.float32))
    b1 = np.asarray(ii["L0_ffn_b1"], np.float32).ravel()
    b2 = np.asarray(ii["L0_ffn_b2"], np.float32).ravel()
    gam = np.asarray(ii["L0_ffn_ln_gamma"], np.float32).ravel()
    bet = np.asarray(ii["L0_ffn_ln_beta"], np.float32).ravel()
    wr = np.ascontiguousarray(np.asarray(ii["L0_ffn_Wr"], np.float32))
    br = np.asarray(ii["L0_ffn_br"], np.float32).reshape(1, 8)

    b1g = np.ascontiguousarray(b1.reshape(32, 128).T)
    b2g = np.ascontiguousarray(b2.reshape(8, 128).T)
    gamg = np.ascontiguousarray(gam.reshape(8, 128).T)
    betg = np.ascontiguousarray(bet.reshape(8, 128).T)

    import ml_dtypes
    idb = np.eye(128, dtype=np.float32).astype(ml_dtypes.bfloat16)
    idf = np.eye(128, dtype=np.float32)

    nc = _get_nc()
    in_maps = []
    for c in range(NCORES):
        m = {
            "x_tm": np.ascontiguousarray(x[c]),
            "wkT": wkT, "wvT": wvT, "wqT": wqT, "w0T": w0T, "w1T": w1T,
            "w1s": w1s, "woutT": woutT, "wg2": wg2, "bg2": bg2,
            "b1g": b1g, "b2g": b2g, "gamg": gamg, "betg": betg,
            "wr": wr, "br": br,
            "w1sl": np.ascontiguousarray(w1f[c * 128:(c + 1) * 128, :]),
            "w1ff": w1f, "w2ff": w2f,
            "w2sl": np.ascontiguousarray(
                w2f[c * 512:(c + 1) * 512, :].reshape(128, 4096)),
            "idb": idb,
        }
        in_maps.append(m)

    res = run_bass_kernel_spmd(nc, in_maps, core_ids=list(range(NCORES)))
    out = np.stack([r["out_tm"] for r in res.results], axis=0)
    if DBG:
        kernel.last_results = res.results
    return out


# version 13 (final)
# speedup vs baseline: 1.1382x; 1.1382x over previous
import numpy as np
import concourse.bass as bass
import concourse.mybir as mybir
from concourse.bass_utils import run_bass_kernel_spmd

F32 = mybir.dt.float32
BF16 = mybir.dt.bfloat16
AF = mybir.ActivationFunctionType
ALU = mybir.AluOpType
AX = mybir.AxisListType

NCORES = 8
TOK = 2048
D = 1024
FF = 4096
NTC = 16
NDC = 8
R_ROUNDS = 32
SCALE_D2 = 2.0 / 4096.0
EPS = 1e-5

KW_BOT = 209715
KW_TOP = 4194304 - 209715 + 1
KTARGETS = [KW_BOT, KW_TOP, KW_BOT, KW_TOP,
            204 * 8, (4096 - 204 + 1) * 8, 51 * 8, (1024 - 51 + 1) * 8]

DBG = True


class Ctr:
    __slots__ = ("sem", "n")

    def __init__(self, sem):
        self.sem = sem
        self.n = 0


class DQ:

    def __init__(self, sems):
        self.sems = sems
        self.i = 0
        self.last = None

    def all(self):
        return [(s, s.n) for s in self.sems]


STOP_AT = None
MARKS = {}


def build_nc(stop_at=None):
    nc = bass.Bass("TRN2", target_bir_lowering=False, debug=False)

    def inp(name, shape, dt=F32):
        return nc.dram_tensor(name, list(shape), dt, kind="ExternalInput").ap()

    x_tm = inp("x_tm", (TOK, D))
    wkT = inp("wkT", (D, D))
    wvT = inp("wvT", (D, D))
    wqT = inp("wqT", (D, D))
    w0T = inp("w0T", (D, D))
    w1T = inp("w1T", (D, D))
    w1s = inp("w1s", (D, D))
    woutT = inp("woutT", (D, D))
    wg2 = inp("wg2", (D, 2))
    bg2 = inp("bg2", (1, 2))
    b1g_i = inp("b1g", (128, 32))
    b2g_i = inp("b2g", (128, 8))
    gamg_i = inp("gamg", (128, 8))
    betg_i = inp("betg", (128, 8))
    wr_i = inp("wr", (D, 8))
    br_i = inp("br", (1, 8))
    w1sl = inp("w1sl", (128, FF))
    w2sl = inp("w2sl", (128, FF))
    w1ff = inp("w1ff", (D, FF))
    w2ff = inp("w2ff", (FF, D))
    idb_i = inp("idb", (128, 128), BF16)

    out_tm = nc.dram_tensor("out_tm", [TOK, D], F32, kind="ExternalOutput").ap()

    def scr(name, shape, dt=F32, shared=False):
        if shared:
            return nc.dram_tensor(name, list(shape), dt, addr_space="Shared")
        return nc.dram_tensor(name, list(shape), dt)

    q_dram = scr("q_dram", (D, TOK), BF16)
    ln0blk = scr("ln0blk", (NDC, TOK, 128), BF16)
    ublk = scr("ublk", (NDC, TOK, 128), BF16)
    y1_dram = scr("y1_dram", (TOK, D))
    absw1_d = scr("absw1_d", (128, FF))
    absw2_d = scr("absw2_d", (128, FF))
    g1_part = scr("g1_part", (D, D))
    g1_sum = scr("g1_sum", (D, D), shared=True)
    g0_part = scr("g0_part", (D, D))
    g0_sum = scr("g0_sum", (D, D), shared=True)
    gate_in = scr("gate_in", (1, 2))
    gate_out = scr("gate_out", (1, 2), shared=True)
    gate_bc = scr("gate_bc", (1, 2))
    cc_in = scr("cc_in", (128, 1))
    cc_out = scr("cc_out", (128, 1), shared=True)
    thr_d = scr("thr_d", (1, 8))
    nthr_d = scr("nthr_d", (1, 8))
    cbc_d = scr("cbc_d", (1, 16))
    pcd1 = scr("pcd1", (1, TOK), BF16)
    pcd2 = scr("pcd2", (1, TOK), BF16)
    pcc_d = scr("pcc_d", (128, 16))
    ppc_d = scr("ppc_d", (128, 16))
    w1t_loc = scr("w1t_loc", (256, 128, 128), BF16)
    w2bf_loc = scr("w2bf_loc", (FF, D), BF16)
    w1p_in = scr("w1p_in", (32, 128, 128), BF16)
    w1p_d = scr("w1p_d", (256, 128, 128), BF16, shared=True)
    pw2_in = scr("pw2_in", (512, D), BF16)
    pw2_d = scr("pw2_d", (FF, D), BF16, shared=True)

    dbg = {}
    if DBG:
        def dout(name, shape, dt=F32):
            dbg[name] = nc.dram_tensor(name, list(shape), dt, kind="ExternalOutput").ap()
        dout("dbg_gates", (1, 4))
        dout("dbg_thr", (1, 8))
        dout("dbg_k", (128, NDC * TOK), BF16)
        dout("dbg_d2", (128, NDC * TOK), BF16)
        dout("dbg_w0nT", (128, NDC * D), BF16)
        dout("dbg_attn", (128, NDC * TOK), BF16)
        dout("dbg_pc", (2, TOK), BF16)
        dout("dbg_actc", (128, 16384), BF16)
        dout("dbg_actp", (128, 16384), BF16)
        dout("dbg_yc", (128, 8192), BF16)
        dout("dbg_fo", (128, 4096), BF16)

    from contextlib import ExitStack
    ctx = ExitStack()

    def sbt(name, shape, dt=F32):
        return ctx.enter_context(nc.sbuf_tensor(name, list(shape), dt))

    slots_f = [sbt(f"slot{i}", [128, 8192], F32) for i in range(5)]
    slots = [s[:, :].bitcast(BF16) for s in slots_f]
    wa_f = sbt("warena", [128, 8192], F32)
    wa_b = wa_f[:, :].bitcast(BF16)
    sbig_f = sbt("sbig", [128, 2048], F32)
    sbig_b = sbig_f[:, :].bitcast(BF16)

    idb = sbt("idb_sb", [128, 128], BF16)
    cnt_p = sbt("cnt_p", [128, 32], F32)
    cnt_t = sbt("cnt_t", [128, 32], F32)
    cnt_r = sbt("cnt_r", [128, 1], F32)
    Lrow = sbt("Lrow", [1, 128], F32)
    Crow = sbt("Crow", [1, 32], F32)
    cnt8 = sbt("cnt8", [1, 8], F32)
    lo8 = sbt("lo8", [1, 8], F32)
    hi8 = sbt("hi8", [1, 8], F32)
    mid8 = sbt("mid8", [1, 8], F32)
    t8a = sbt("t8a", [1, 8], F32)
    t8b = sbt("t8b", [1, 8], F32)
    ktar = sbt("ktar", [1, 8], F32)
    thrb = sbt("thrb", [128, 8], F32)
    thrf = sbt("thrf", [128, 8], F32)
    nthrf = sbt("nthrf", [128, 8], F32)
    mx_p = sbt("mx_p", [128, 32], F32)
    b1gs = sbt("b1gs", [128, 32], F32)
    b2gs = sbt("b2gs", [128, 8], F32)
    b1a = sbt("b1a", [128, 32], F32)
    b2a = sbt("b2a", [128, 8], F32)
    pb1g = sbt("pb1g", [128, 32], F32)
    pb2g = sbt("pb2g", [128, 8], F32)
    tb1 = sbt("tb1", [128, 32], F32)
    tb2 = sbt("tb2", [128, 32], F32)
    wgb = sbt("wgb", [128, 16], BF16)
    wgf = sbt("wgf", [128, 16], F32)
    bg_sb = sbt("bg_sb", [1, 2], F32)
    gacc = sbt("gacc", [2, 4], F32)
    grow = sbt("grow", [1, 2], F32)
    gate2 = sbt("gate2", [1, 2], F32)
    gate4 = sbt("gate4", [1, 4], F32)
    gal = sbt("gal", [128, 2], F32)
    stat = sbt("stat", [128, 16], F32)
    stat2 = sbt("stat2", [128, 16], F32)
    mcol = sbt("mcol", [128, 16], F32)
    vcol = sbt("vcol", [128, 16], F32)
    rcol = sbt("rcol", [128, 16], F32)
    muU = sbt("muU", [128, 16], F32)
    rU = sbt("rU", [128, 16], F32)
    rmU = sbt("rmU", [128, 16], F32)
    wr_sb = sbt("wr_sb", [128, 64], F32)
    wrp = sbt("wrp", [128, 64], BF16)
    wrb = sbt("wrb", [128, 64], BF16)
    gam_sb = sbt("gam_sb", [128, 8], F32)
    bet_sb = sbt("bet_sb", [128, 8], F32)
    br_sb = sbt("br_sb", [1, 8], F32)
    c1r = sbt("c1r", [1, 8], F32)
    c2r = sbt("c2r", [1, 8], F32)
    c12bc = sbt("c12bc", [128, 16], F32)
    lgt = sbt("lgt", [128, 8], F32)
    lg2 = sbt("lg2", [128, 8], F32)
    lg3 = sbt("lg3", [128, 8], F32)
    expv = sbt("expv", [128, 8], F32)
    sm = [sbt(f"sm{i}", [128, 1], F32) for i in range(6)]
    pcc = sbt("pcc", [128, 16], F32)
    ppc = sbt("ppc", [128, 16], F32)
    onesb = sbt("onesb", [128, 1], BF16)
    epsb = sbt("epsb", [128, 1], F32)
    trash = sbt("trash", [128, 1024], BF16)

    pm = [ctx.enter_context(nc.psum_tensor(f"pm{i}", [128, 512], F32)) for i in range(4)]
    pt = [ctx.enter_context(nc.psum_tensor(f"pt{i}", [128, 1024], BF16)) for i in range(2)]
    px = ctx.enter_context(nc.psum_tensor("px", [128, 512], F32))
    pr = ctx.enter_context(nc.psum_tensor("pr", [128, 512], F32))

    SD = DQ([Ctr(ctx.enter_context(nc.semaphore(f"SD{i}"))) for i in range(6)])
    SC = DQ([Ctr(ctx.enter_context(nc.semaphore(f"SC{i}"))) for i in range(6)])
    GD = DQ([Ctr(ctx.enter_context(nc.semaphore(f"GD{i}"))) for i in range(2)])
    MM = Ctr(ctx.enter_context(nc.semaphore("MM")))
    VE = Ctr(ctx.enter_context(nc.semaphore("VE")))
    SE = Ctr(ctx.enter_context(nc.semaphore("SE")))
    CC = Ctr(ctx.enter_context(nc.semaphore("CC")))

    plan = {k: [] for k in ("sync", "scalar", "vector", "tensor", "gpsimd")}
    seqc = [0]
    marks = {}

    def mark(name):
        marks[name] = seqc[0]

    def E(eng, build=None, waits=(), inc=None, inc_k=1):
        wl = [(c.sem, v) for (c, v) in waits]
        tok = None
        if isinstance(inc, DQ) and build is not None:
            s = inc.sems[inc.i % len(inc.sems)]
            inc.i += 1
            wl = [(s.sem, s.n)] + wl
            s.n += 16
            tok = (s, s.n)
            inc.last = tok
            sem, inc_k = s.sem, 16
        else:
            sem = inc.sem if inc is not None else None

        def fn(e, _w=wl, _b=build, _s=sem, _k=inc_k):
            for (s_, v) in _w:
                e.wait_ge(s_, v)
            if _b is not None:
                ins = _b(e)
                if _s is not None:
                    ins.then_inc(_s, _k)

        fn._seq = seqc[0]
        seqc[0] += 1
        plan[eng].append(fn)
        if tok is not None:
            return tok
        if inc is not None and build is not None:
            inc.n += inc_k
            return inc.n
        return None

    bank_free = {}

    def bwait(bank):
        w = bank_free.get(id(bank))
        return [w] if w is not None else []

    def bset(bank, ctr, val):
        bank_free[id(bank)] = (ctr, val)

    def fm(si, c, c0, n):
        return slots[si][:, c * TOK + c0: c * TOK + c0 + n]

    def tm(si, c, c0, n):
        return slots[si][:, c * D + c0: c * D + c0 + n]

    tok_idb = E("sync", lambda e: e.dma_start(out=idb[:], in_=idb_i), inc=SD)
    tok_b1g = E("sync", lambda e: e.dma_start(out=b1gs[:], in_=b1g_i), inc=SD)
    tok_b2g = E("sync", lambda e: e.dma_start(out=b2gs[:], in_=b2g_i), inc=SD)
    tok_bg = E("sync", lambda e: e.dma_start(out=bg_sb[:], in_=bg2), inc=SD)
    tok_wg = E("sync", lambda e: e.dma_start(
        out=wgf[:].rearrange("p (c t) -> p c t", c=8),
        in_=wg2.rearrange("(c p) t -> p c t", p=128)), inc=SD)
    E("scalar", lambda e: e.copy(wgb[:], wgf[:]), waits=[tok_wg], inc=SE)
    se_wgb = SE.n
    sd1 = E("sync", lambda e: e.dma_start(out=wa_f[:, 0:4096], in_=w1sl), inc=SD)
    E("scalar", lambda e: e.activation(wa_f[:, 4096:8192], wa_f[:, 0:4096], AF.Abs),
      waits=[sd1], inc=SE)
    se1 = SE.n
    E("vector", lambda e: e.reduce_max(mx_p[:, 0:1], wa_f[:, 4096:8192], axis=AX.X),
      waits=[(SE, se1)], inc=VE)
    E("vector", lambda e: e.reduce_max(mx_p[:, 1:2], wa_f[:, 4096:8192], axis=AX.X), inc=VE)
    ve_w1m = VE.n
    sc_a1 = E("scalar", lambda e: e.dma_start(out=absw1_d.ap(), in_=wa_f[:, 4096:8192]),
      waits=[(SE, se1)], inc=SC)
    sd2 = E("sync", lambda e: e.dma_start(out=wa_f[:, 0:4096], in_=w2sl),
      waits=[sc_a1, (VE, ve_w1m)], inc=SD)
    E("scalar", lambda e: e.activation(wa_f[:, 4096:8192], wa_f[:, 0:4096], AF.Abs),
      waits=[sd2, sc_a1], inc=SE)
    se2 = SE.n
    E("vector", lambda e: e.reduce_max(mx_p[:, 2:3], wa_f[:, 4096:8192], axis=AX.X),
      waits=[(SE, se2)], inc=VE)
    E("vector", lambda e: e.reduce_max(mx_p[:, 3:4], wa_f[:, 4096:8192], axis=AX.X), inc=VE)
    ve_w2m = VE.n
    sc_a2 = E("scalar", lambda e: e.dma_start(out=absw2_d.ap(), in_=wa_f[:, 4096:8192]),
      waits=[(SE, se2)], inc=SC)

    E("scalar", lambda e: e.activation(b1a[:], b1gs[:], AF.Abs), waits=[tok_b1g, tok_b2g, tok_idb, tok_bg], inc=SE)
    E("scalar", lambda e: e.activation(b2a[:], b2gs[:], AF.Abs), inc=SE)
    se_babs = SE.n

    def _init(e):
        for i, kv in enumerate(KTARGETS):
            e.memset(ktar[0:1, i:i + 1], float(kv))
        e.memset(epsb[:], EPS)
        return e.memset(lo8[:], 0.0)
    E("vector", _init, inc=VE)
    E("vector", lambda e: e.memset(cnt_p[:], 0.0), inc=VE)
    E("vector", lambda e: e.reduce_max(mx_p[:, 4:5], b1a[:], axis=AX.X),
      waits=[(SE, se_babs)], inc=VE)
    E("vector", lambda e: e.reduce_max(mx_p[:, 5:6], b1a[:], axis=AX.X), inc=VE)
    E("vector", lambda e: e.reduce_max(mx_p[:, 6:7], b2a[:], axis=AX.X), inc=VE)
    E("vector", lambda e: e.reduce_max(mx_p[:, 7:8], b2a[:], axis=AX.X), inc=VE)
    E("vector", lambda e: e.memset(mx_p[:, 8:32], 0.0), inc=VE)
    ve_mx = VE.n
    E("vector", lambda e: e.transpose(cnt_t[:], mx_p[:]), waits=[(VE, ve_mx)], inc=VE)
    v = VE.n
    E("vector", lambda e: e.reduce_max(cnt_r[:], cnt_t[:], axis=AX.X), waits=[(VE, v)], inc=VE)
    ve_r = VE.n
    sc0 = E("scalar", lambda e: e.dma_start(out=cc_in.ap(), in_=cnt_r[:]),
      waits=[(VE, ve_r)], inc=SC)
    E("gpsimd", lambda e: e.collective_compute(
        "AllReduce", ALU.max, replica_groups=[list(range(NCORES))],
        ins=[cc_in.ap().opt()], outs=[cc_out.ap().opt()]),
      waits=[sc0], inc=CC)
    cc_hi = CC.n
    sd_L = E("sync", lambda e: e.dma_start(out=Lrow[:], in_=cc_out.ap().rearrange("p a -> a p")),
      waits=[(CC, cc_hi)], inc=SD)
    E("vector", lambda e: e.reduce_max(
        hi8[:], Lrow[:].rearrange("a (r i) -> a i r", r=4)[:, 0:8, :], axis=AX.X),
      waits=[sd_L], inc=VE)
    v = VE.n
    E("vector", lambda e: e.tensor_add(mid8[:], lo8[:], hi8[:]), waits=[(VE, v)], inc=VE)
    v = VE.n
    E("vector", lambda e: e.tensor_scalar(mid8[:], mid8[:], 0.5, None, op0=ALU.mult),
      waits=[(VE, v)], inc=VE)
    ve_mid0 = VE.n

    sstate = {"buf0": 0, "buf1": 0, "ve_mid": ve_mid0}
    pump_state = {"gen": None, "acc": 0.0, "done": False, "spacing": 18.0}

    def pump(w=2.0):
        ps = pump_state
        if ps["done"] or ps["gen"] is None:
            return
        ps["acc"] += w
        while ps["acc"] >= ps["spacing"] and not ps["done"]:
            try:
                next(ps["gen"])
                ps["acc"] -= ps["spacing"]
            except StopIteration:
                ps["done"] = True

    def pump_flush():
        ps = pump_state
        if ps["gen"] is None:
            return
        while not ps["done"]:
            try:
                next(ps["gen"])
            except StopIteration:
                ps["done"] = True

    def emit_thr_bcast(src_row, dst, ve_ready):
        prev = sstate.get("thrd_prev")
        s = E("gpsimd", lambda e: e.dma_start(out=thr_d.ap(), in_=src_row[:]),
              waits=[(VE, ve_ready)] + ([prev] if prev else []), inc=GD)
        tok = E("gpsimd", lambda e: e.dma_start(
            out=dst[:], in_=thr_d.ap().broadcast_to([128, 8])),
          waits=[s], inc=GD)
        sstate["thrd_prev"] = tok
        return tok

    def emit_search_round():
        yield from _emit_search_round()

    def _emit_search_round():
        sc_thrb = sstate["sc_thrb"]
        for a, src in enumerate((absw1_d, absw2_d)):
            for c in range(4):
                bslice = sbig_f[:, (c % 2) * 1024:(c % 2) * 1024 + 1024]
                wv = sstate[f"buf{c % 2}"]
                sd_c = E("gpsimd", lambda e, s=src, c=c, b=bslice: e.dma_start(
                    out=b, in_=s.ap()[:, c * 1024:(c + 1) * 1024]),
                  waits=[(VE, wv), sc_thrb], inc=GD)
                for ti in range(2):
                    col = a * 8 + ti * 4 + c
                    thc = a * 2 + ti
                    E("vector", lambda e, b=bslice, col=col, thc=thc: e.tensor_scalar(
                        trash[:, 0:1024], b, thrb[:, thc:thc + 1], None,
                        op0=ALU.is_le, op1=ALU.add,
                        accum_out=cnt_p[:, col:col + 1]),
                      waits=[sd_c, (VE, VE.n)], inc=VE)
                sstate[f"buf{c % 2}"] = VE.n
        for arr, col, thc in ((b1a, 16, 4), (b1a, 17, 5), (b2a, 18, 6), (b2a, 19, 7)):
            E("vector", lambda e, arr=arr, col=col, thc=thc, w=arr.shape[1]: e.tensor_scalar(
                trash[:, 0:w], arr[:], thrb[:, thc:thc + 1], None,
                op0=ALU.is_le, op1=ALU.add, accum_out=cnt_p[:, col:col + 1]),
              waits=[(VE, VE.n)], inc=VE)
        ve_cmp = VE.n
        E("vector", lambda e: e.transpose(cnt_t[:], cnt_p[:]), waits=[(VE, ve_cmp)], inc=VE)
        v1 = VE.n
        E("vector", lambda e: e.reduce_sum(cnt_r[:], cnt_t[:], axis=AX.X),
          waits=[(VE, v1)], inc=VE)
        v2 = VE.n
        s1 = E("gpsimd", lambda e: e.dma_start(out=cc_in.ap(), in_=cnt_r[:]),
          waits=[(VE, v2)], inc=GD)
        E("gpsimd", lambda e: e.collective_compute(
            "AllReduce", ALU.add, replica_groups=[list(range(NCORES))],
            ins=[cc_in.ap().opt()], outs=[cc_out.ap().opt()]),
          waits=[s1], inc=CC)
        c1 = CC.n
        yield
        sdn = E("gpsimd", lambda e: e.dma_start(out=Lrow[:], in_=cc_out.ap().rearrange("p a -> a p")),
          waits=[(CC, c1)], inc=GD)
        E("vector", lambda e: e.reduce_sum(
            Crow[:], Lrow[:].rearrange("a (r i) -> a i r", r=4), axis=AX.X),
          waits=[sdn, (VE, VE.n)], inc=VE)
        v3 = VE.n
        E("vector", lambda e: e.reduce_sum(
            cnt8[0:1, 0:4], Crow[:, 0:16].rearrange("a (s c) -> a s c", c=4), axis=AX.X),
          waits=[(VE, v3)], inc=VE)
        v4 = VE.n
        E("vector", lambda e: e.tensor_copy(cnt8[0:1, 4:8], Crow[:, 16:20]),
          waits=[(VE, v4)], inc=VE)
        v5 = VE.n
        E("vector", lambda e: e.tensor_tensor(t8a[:], cnt8[:], ktar[:], op=ALU.is_ge),
          waits=[(VE, v5)], inc=VE)
        v6 = VE.n
        steps = [
            lambda e: e.tensor_sub(t8b[:], mid8[:], hi8[:]),
            lambda e: e.tensor_mul(t8b[:], t8b[:], t8a[:]),
            lambda e: e.tensor_add(hi8[:], hi8[:], t8b[:]),
            lambda e: e.tensor_sub(t8b[:], mid8[:], lo8[:]),
            lambda e: e.tensor_mul(t8a[:], t8b[:], t8a[:]),
            lambda e: e.tensor_sub(t8b[:], t8b[:], t8a[:]),
            lambda e: e.tensor_add(lo8[:], lo8[:], t8b[:]),
            lambda e: e.tensor_add(mid8[:], lo8[:], hi8[:]),
            lambda e: e.tensor_scalar(mid8[:], mid8[:], 0.5, None, op0=ALU.mult),
        ]
        for st in steps:
            E("vector", st, waits=[(VE, VE.n)], inc=VE)
        sstate["sc_thrb"] = emit_thr_bcast(mid8, thrb, VE.n)

    def emit_search_extract():
        sstate["sc_thrb"] = emit_thr_bcast(hi8, thrb, VE.n)
        yield
        E("vector", lambda e: e.memset(mx_p[:], 0.0), waits=[sstate["sc_thrb"]], inc=VE)
        for a, src in enumerate((absw1_d, absw2_d)):
            for c in range(4):
                bload = sbig_f[:, 0:1024]
                bmask = sbig_f[:, 1024:2048]
                sd_c = E("gpsimd", lambda e, s=src, c=c, b=bload: e.dma_start(
                    out=b, in_=s.ap()[:, c * 1024:(c + 1) * 1024]),
                  waits=[(VE, VE.n)], inc=GD)
                for ti in range(2):
                    col = a * 8 + ti * 4 + c
                    thc = a * 2 + ti
                    E("vector", lambda e, b=bload, m=bmask, thc=thc: e.scalar_tensor_tensor(
                        m, b, thrb[:, thc:thc + 1], b, op0=ALU.is_le, op1=ALU.mult),
                      waits=[sd_c, (VE, VE.n)], inc=VE)
                    v1 = VE.n
                    E("vector", lambda e, m=bmask, col=col: e.reduce_max(
                        mx_p[:, col:col + 1], m, axis=AX.X),
                      waits=[(VE, v1)], inc=VE)
                yield
        for arr, col, thc in ((b1a, 16, 4), (b1a, 17, 5), (b2a, 18, 6), (b2a, 19, 7)):
            E("vector", lambda e, arr=arr, thc=thc, w=arr.shape[1]: e.scalar_tensor_tensor(
                tb1[:, 0:w], arr[:], thrb[:, thc:thc + 1], arr[:],
                op0=ALU.is_le, op1=ALU.mult),
              waits=[(VE, VE.n)], inc=VE)
            v1 = VE.n
            E("vector", lambda e, col=col, w=arr.shape[1]: e.reduce_max(
                mx_p[:, col:col + 1], tb1[:, 0:w], axis=AX.X),
              waits=[(VE, v1)], inc=VE)
        ve_xm = VE.n
        E("vector", lambda e: e.transpose(cnt_t[:], mx_p[:]), waits=[(VE, ve_xm)], inc=VE)
        v1 = VE.n
        E("vector", lambda e: e.reduce_max(cnt_r[:], cnt_t[:], axis=AX.X),
          waits=[(VE, v1)], inc=VE)
        v2 = VE.n
        s1 = E("gpsimd", lambda e: e.dma_start(out=cc_in.ap(), in_=cnt_r[:]),
          waits=[(VE, v2)], inc=GD)
        E("gpsimd", lambda e: e.collective_compute(
            "AllReduce", ALU.max, replica_groups=[list(range(NCORES))],
            ins=[cc_in.ap().opt()], outs=[cc_out.ap().opt()]),
          waits=[s1], inc=CC)
        c1 = CC.n
        sdn = E("gpsimd", lambda e: e.dma_start(out=Lrow[:], in_=cc_out.ap().rearrange("p a -> a p")),
          waits=[(CC, c1)], inc=GD)
        E("vector", lambda e: e.reduce_max(
            Crow[:], Lrow[:].rearrange("a (r i) -> a i r", r=4), axis=AX.X),
          waits=[sdn, (VE, VE.n)], inc=VE)
        v3 = VE.n
        E("vector", lambda e: e.reduce_max(
            t8a[0:1, 0:4], Crow[:, 0:16].rearrange("a (s c) -> a s c", c=4), axis=AX.X),
          waits=[(VE, v3)], inc=VE)
        v4 = VE.n
        E("vector", lambda e: e.tensor_copy(t8a[0:1, 4:8], Crow[:, 16:20]),
          waits=[(VE, v4)], inc=VE)
        ve_thr = VE.n
        s2 = E("gpsimd", lambda e: e.dma_start(out=thr_d.ap(), in_=t8a[:]),
          waits=[(VE, ve_thr)], inc=GD)
        tf1 = E("gpsimd", lambda e: e.dma_start(out=thrf[:], in_=thr_d.ap().broadcast_to([128, 8])),
          waits=[s2], inc=GD)
        E("vector", lambda e: e.tensor_scalar(t8b[:], t8a[:], -1.0, None, op0=ALU.mult),
          waits=[(VE, ve_thr)], inc=VE)
        vn = VE.n
        s3 = E("gpsimd", lambda e: e.dma_start(out=nthr_d.ap(), in_=t8b[:]),
          waits=[(VE, vn)], inc=GD)
        tf2 = E("gpsimd", lambda e: e.dma_start(out=nthrf[:], in_=nthr_d.ap().broadcast_to([128, 8])),
          waits=[s3], inc=GD)
        sstate["sc_thrf"] = [tf1, tf2]
        if DBG:
            E("scalar", lambda e: e.dma_start(out=dbg["dbg_thr"], in_=t8a[:]),
              waits=[(VE, ve_thr)], inc=SC)

    mark('ph0')
    xb = [wa_f[:, 0:1024], wa_f[:, 1024:2048]]
    sqb = wa_f[:, 2048:3072]
    sd_x = {}
    ve_sq_done = 0
    se_prev_sq = 0
    for c in range(NTC):
        buf = xb[c % 2]
        wv = sd_x.get(f"used{c % 2}", 0)
        sdc = E("sync", lambda e, c=c, b=buf: e.dma_start(
            out=b, in_=x_tm[c * 128:(c + 1) * 128, :]),
          waits=[(VE, wv), sc_a2], inc=SD)
        E("vector", lambda e, b=buf, c=c: e.reduce_sum(stat[:, c:c + 1], b, axis=AX.X),
          waits=[sdc], inc=VE)
        E("scalar", lambda e, b=buf: e.activation(sqb, b, AF.Square),
          waits=[sdc, (VE, ve_sq_done)], inc=SE)
        sec = SE.n
        E("vector", lambda e, c=c: e.reduce_sum(stat2[:, c:c + 1], sqb, axis=AX.X),
          waits=[(SE, sec)], inc=VE)
        ve_sq_done = VE.n
        sd_x[f"used{c % 2}"] = VE.n

    def stats_chain(ve_in, s1t, s2t, mdst, rdst):
        E("vector", lambda e: e.tensor_scalar(mdst[:], s1t[:], 1.0 / D, None, op0=ALU.mult),
          waits=[(VE, ve_in)], inc=VE)
        v1 = VE.n
        E("vector", lambda e: e.tensor_mul(vcol[:], mdst[:], mdst[:]),
          waits=[(VE, v1)], inc=VE)
        v2 = VE.n
        E("vector", lambda e: e.scalar_tensor_tensor(
            vcol[:], s2t[:], 1.0 / D, vcol[:], op0=ALU.mult, op1=ALU.subtract),
          waits=[(VE, v2)], inc=VE)
        v3 = VE.n
        E("scalar", lambda e: e.activation(rdst[:], vcol[:], AF.Sqrt, bias=epsb[:]),
          waits=[(VE, v3)], inc=SE)
        s1 = SE.n
        E("vector", lambda e: e.reciprocal(rdst[:], rdst[:]), waits=[(SE, s1)], inc=VE)
        return VE.n

    ve_ln0stats = stats_chain(VE.n, stat, stat2, mcol, rcol)

    lnb = [wa_b[:, 6144:7168], wa_b[:, 7168:8192]]
    sc_ln0 = 0
    for c in range(NTC):
        buf = xb[c % 2]
        wv = sd_x.get(f"u2{c % 2}", 0)
        sdc = E("sync", lambda e, c=c, b=buf: e.dma_start(
            out=b, in_=x_tm[c * 128:(c + 1) * 128, :]),
          waits=[(VE, wv), (VE, ve_ln0stats)], inc=SD)
        ob = lnb[c % 2]
        wsc = sd_x.get(f"lnb{c % 2}")
        E("vector", lambda e, b=buf, o=ob, c=c: e.tensor_scalar(
            o, b, mcol[:, c:c + 1], rcol[:, c:c + 1], op0=ALU.subtract, op1=ALU.mult),
          waits=[sdc] + ([wsc] if wsc else []), inc=VE)
        vc = VE.n
        sd_x[f"u2{c % 2}"] = vc
        sd_x[f"lnb{c % 2}"] = E("scalar", lambda e, o=ob, c=c: e.dma_start(
            out=ln0blk.ap()[:, c * 128:(c + 1) * 128, :].rearrange("g p d -> p g d"),
            in_=o.rearrange("p (g d) -> p g d", g=8)),
          waits=[(VE, vc)], inc=SC)
    sc_ln0 = SC.all()

    mark('A')
    for dc in range(NDC):
        E("scalar", lambda e, dc=dc: e.dma_start(
            out=fm(0, dc, 0, TOK), in_=ln0blk.ap()[dc], transpose=True),
          waits=sc_ln0, inc=SC)
    sc_ln0fm = SC.all()

    cast_stage = [slots_f[3][:, 0:1024], slots_f[3][:, 1024:2048]]
    cast_stage2 = [slots_f[2][:, 1536:2560], slots_f[2][:, 2560:3584]]
    cast_state = {"s0": 0, "s1": 0}

    def cast_weight(src, dst_bf, pre_waits=(), stage=None):
        stg = stage if stage is not None else cast_stage
        for c in range(NDC):
            st = stg[c % 2]
            wv = cast_state[f"s{c % 2}"]
            sdc = E("sync", lambda e, c=c, s=st, src=src: e.dma_start(
                out=s, in_=src[c * 128:(c + 1) * 128, :]),
              waits=[(SE, wv)] + list(pre_waits), inc=SD)
            E("scalar", lambda e, c=c, s=st, d=dst_bf: e.copy(
                d[:, c * 1024:(c + 1) * 1024], s),
              waits=[sdc], inc=SE)
            cast_state[f"s{c % 2}"] = SE.n
            pump(1.2)
        return SE.n

    RW0 = wa_b[:, 0:8192]
    RW1 = wa_b[:, 8192:16384]
    se_wk = cast_weight(wkT, RW0)
    se_wv = cast_weight(wvT, RW1)

    ve_gacc = 0
    for ts4 in range(4):
        w = bwait(px)
        for dc in range(NDC):
            E("tensor", lambda e, dc=dc, ts4=ts4: e.matmul(
                px[0:2, :], wgb[:, dc * 2:dc * 2 + 2], fm(0, dc, ts4 * 512, 512),
                start=(dc == 0), stop=(dc == 7)),
              waits=((sc_ln0fm + [(SE, se_wgb)]) if (ts4 == 0 and dc == 0) else []) + (w if dc == 0 else []),
              inc=(MM if dc == 7 else None))
        mmg = MM.n
        E("vector", lambda e, ts4=ts4: e.reduce_sum(gacc[:, ts4:ts4 + 1], px[0:2, :], axis=AX.X),
          waits=[(MM, mmg), (VE, ve_gacc)], inc=VE)
        ve_gacc = VE.n
        bset(px, VE, ve_gacc)
    E("vector", lambda e: e.reduce_sum(sm[0][0:2, :], gacc[0:2, :], axis=AX.X),
      waits=[(VE, ve_gacc)], inc=VE)
    ve_gpart = VE.n
    sc_gin = E("scalar", lambda e: e.dma_start(out=gate_in.ap().rearrange("a t -> t a"), in_=sm[0][0:2, :]),
      waits=[(VE, ve_gpart)], inc=SC)
    E("gpsimd", lambda e: e.collective_compute(
        "AllReduce", ALU.add, replica_groups=[list(range(NCORES))],
        ins=[gate_in.ap().opt()], outs=[gate_out.ap().opt()]),
      waits=[sc_gin], inc=CC)
    cc_gate = CC.n
    sd_gate = E("sync", lambda e: e.dma_start(out=grow[:], in_=gate_out.ap()),
      waits=[(CC, cc_gate)], inc=SD)
    E("vector", lambda e: e.tensor_scalar(grow[:], grow[:], 1.0 / 16384.0, None, op0=ALU.mult),
      waits=[sd_gate], inc=VE)
    v1 = VE.n
    E("vector", lambda e: e.tensor_add(grow[:], grow[:], bg_sb[:]), waits=[(VE, v1)], inc=VE)
    v2 = VE.n
    E("scalar", lambda e: e.activation(gate2[:], grow[:], AF.Sigmoid), waits=[(VE, v2)], inc=SE)
    se_g = SE.n
    E("vector", lambda e: e.tensor_copy(gate4[0:1, 0:2], gate2[:]), waits=[(SE, se_g)], inc=VE)
    E("vector", lambda e: e.tensor_scalar(gate4[0:1, 2:3], gate2[0:1, 0:1], -1.0, 1.0,
                                          op0=ALU.mult, op1=ALU.add), waits=[(VE, VE.n)], inc=VE)
    E("vector", lambda e: e.tensor_scalar(gate4[0:1, 3:4], gate2[0:1, 1:2], -1.0, None,
                                          op0=ALU.mult), waits=[(VE, VE.n)], inc=VE)
    ve_gate4 = VE.n
    s1 = E("scalar", lambda e: e.dma_start(out=gate_bc.ap(), in_=gate4[0:1, 2:4]),
      waits=[(VE, ve_gate4)], inc=SC)
    sc_gal = E("scalar", lambda e: e.dma_start(out=gal[:], in_=gate_bc.ap().broadcast_to([128, 2])),
      waits=[s1], inc=SC)
    if DBG:
        E("scalar", lambda e: e.dma_start(out=dbg["dbg_gates"], in_=gate4[:]),
          waits=[(VE, ve_gate4)], inc=SC)

    rot = {"i": 0}

    def next_bank():
        b = pm[rot["i"] % 4]
        rot["i"] += 1
        return b

    def mm_fm_set(wreg, src_si, evac, extra_w=(), nk=NDC, rhs_fn=None):
        for oc in range(NDC):
            for ts4 in range(4):
                bank = next_bank()
                w = bwait(bank)
                for kc in range(nk):
                    rhs = rhs_fn(kc, ts4) if rhs_fn else fm(src_si, kc, ts4 * 512, 512)
                    E("tensor", lambda e, b=bank, kc=kc, oc=oc, r=rhs: e.matmul(
                        b[:], wreg[:, kc * 1024 + oc * 128: kc * 1024 + oc * 128 + 128],
                        r, start=(kc == 0), stop=(kc == nk - 1)),
                      waits=(w if kc == 0 else []) + (list(extra_w) if (kc == 0) else []),
                      inc=(MM if kc == nk - 1 else None))
                free = evac(oc, ts4, bank, MM.n)
                bset(bank, *free)
                pump(2.5)

    def search_gen():
        sstate["sc_thrb"] = emit_thr_bcast(mid8, thrb, ve_mid0)
        for r in range(R_ROUNDS):
            yield from emit_search_round()
            yield
        yield from emit_search_extract()

    pump_state["gen"] = search_gen()

    def evac_to_slot(si, func=None):
        def f(oc, ts4, bank, mmv):
            if func is None:
                E("scalar", lambda e, b=bank, oc=oc, ts4=ts4: e.copy(
                    fm(si, oc, ts4 * 512, 512), b[:]), waits=[(MM, mmv)], inc=SE)
            else:
                E("scalar", lambda e, b=bank, oc=oc, ts4=ts4: e.activation(
                    fm(si, oc, ts4 * 512, 512), b[:], func), waits=[(MM, mmv)], inc=SE)
            return (SE, SE.n)
        return f

    mm_fm_set(RW0, 0, evac_to_slot(1), extra_w=[(SE, se_wk)] + sc_ln0fm)
    mm_k_done = MM.n
    se_k_done = SE.n
    mm_fm_set(RW1, 0, evac_to_slot(2), extra_w=[(SE, se_wv)])
    mm_v_done = MM.n
    if DBG:
        E("scalar", lambda e: e.dma_start(out=dbg["dbg_k"], in_=slots[1][:]),
          waits=[(SE, se_k_done)], inc=SC)

    se_wq = cast_weight(wqT, RW0, pre_waits=[(MM, mm_k_done)])

    qst = [slots_f[4][:, 0:256].bitcast(BF16), slots_f[4][:, 256:512].bitcast(BF16)]
    qkey = {}

    def evac_q(oc, ts4, bank, mmv):
        i = (oc * 4 + ts4) % 2
        st = qst[i]
        wq = qkey.get(str(i))
        E("scalar", lambda e, b=bank, s=st: e.copy(s, b[:]),
          waits=[(MM, mmv)] + ([wq] if wq else []), inc=SE)
        sev = SE.n
        qkey[str(i)] = E("scalar", lambda e, s=st, oc=oc, ts4=ts4: e.dma_start(
            out=q_dram.ap()[oc * 128:(oc + 1) * 128, ts4 * 512:(ts4 + 1) * 512], in_=s),
          waits=[(SE, sev)], inc=SC)
        return (SE, sev)

    mm_fm_set(RW0, 0, evac_q, extra_w=[(SE, se_wq)])
    mm_q_done = MM.n
    sc_q_done = SC.all()

    se_w0 = cast_weight(w0T, RW1, pre_waits=[(MM, mm_v_done)])
    se_w1T = cast_weight(w1T, RW0, pre_waits=[(MM, mm_q_done)])

    mark('B')
    d2tmp = slots_f[4][:, 2048:2560]
    ve_d2tmp = 0
    for sl in range(4):
        for ic in range(NDC):
            bank = next_bank()
            w = bwait(bank)
            for jc in range(NDC):
                E("tensor", lambda e, b=bank, jc=jc, ic=ic, sl=sl: e.matmul(
                    b[:], RW1[:, jc * 1024 + ic * 128: jc * 1024 + ic * 128 + 128],
                    fm(1, jc, sl * 512, 512), start=(jc == 0), stop=(jc == 7)),
                  waits=(w if jc == 0 else []) + (([(SE, se_w0)] + sc_q_done) if jc == 0 else []),
                  inc=(MM if jc == 7 else None))
            mmv = MM.n
            E("scalar", lambda e, b=bank, ic=ic: e.activation(
                slots[4][:, ic * 512:(ic + 1) * 512], b[:], AF.Silu),
              waits=[(MM, mmv)], inc=SE)
            bset(bank, SE, SE.n)
            pump(2.5)
        se_a1s = SE.n
        for dc in range(NDC):
            bank = next_bank()
            w = bwait(bank)
            for ec in range(NDC):
                E("tensor", lambda e, b=bank, ec=ec, dc=dc: e.matmul(
                    b[:], RW0[:, ec * 1024 + dc * 128: ec * 1024 + dc * 128 + 128],
                    slots[4][:, ec * 512:(ec + 1) * 512],
                    start=(ec == 0), stop=(ec == 7)),
                  waits=(w if ec == 0 else []) + ([(SE, se_w1T), (SE, se_a1s)] if ec == 0 else []),
                  inc=(MM if ec == 7 else None))
            mmv = MM.n
            E("vector", lambda e, b=bank, dc=dc, sl=sl: e.tensor_sub(
                d2tmp, b[:], fm(2, dc, sl * 512, 512)),
              waits=[(MM, mmv), (VE, ve_d2tmp)], inc=VE)
            v1 = VE.n
            bset(bank, VE, v1)
            E("vector", lambda e: e.tensor_scalar(
                d2tmp, d2tmp, 5.0, -5.0, op0=ALU.min, op1=ALU.max),
              waits=[(VE, v1)], inc=VE)
            v2 = VE.n
            E("vector", lambda e, dc=dc, sl=sl: e.tensor_scalar(
                fm(0, dc, sl * 512, 512), d2tmp, SCALE_D2, None, op0=ALU.mult),
              waits=[(VE, v2)], inc=VE)
            ve_d2tmp = VE.n
            pump(2.5)
    mm_pred_done = MM.n
    ve_d2_done = VE.n
    if DBG:
        E("scalar", lambda e: e.dma_start(out=dbg["dbg_d2"], in_=slots[0][:]),
          waits=[(VE, ve_d2_done)], inc=SC)

    mark('Cp')
    stmp = slots_f[2][:, 0:512]
    utmp = slots_f[2][:, 512:1024]
    wtmp = slots_f[2][:, 1024:1536]
    ve_c = 0
    se_c = 0
    for tc in range(NTC):
        for ih in range(2):
            bank = next_bank()
            w = bwait(bank)
            for jc in range(NDC):
                E("tensor", lambda e, b=bank, jc=jc, tc=tc, ih=ih: e.matmul(
                    b[:], fm(1, jc, tc * 128, 128),
                    RW1[:, jc * 1024 + ih * 512: jc * 1024 + ih * 512 + 512],
                    start=(jc == 0), stop=(jc == 7)),
                  waits=(w if jc == 0 else []) + ([(VE, ve_d2_done)] if (tc == 0 and ih == 0 and jc == 0) else []),
                  inc=(MM if jc == 7 else None))
            mmv = MM.n
            E("scalar", lambda e, b=bank, tc=tc, ih=ih: e.activation(
                tm(3, tc, ih * 512, 512), b[:], AF.Silu), waits=[(MM, mmv)], inc=SE)
            E("scalar", lambda e, b=bank: e.activation(stmp, b[:], AF.Sigmoid),
              waits=[(VE, ve_c)], inc=SE)
            se1_ = SE.n
            E("vector", lambda e: e.tensor_scalar(utmp, stmp, -1.0, 1.0,
                                                  op0=ALU.mult, op1=ALU.add),
              waits=[(SE, se1_), (VE, ve_c)], inc=VE)
            v1 = VE.n
            E("vector", lambda e, b=bank: e.tensor_mul(wtmp, b[:], utmp),
              waits=[(VE, v1)], inc=VE)
            v2 = VE.n
            bset(bank, VE, v2)
            E("vector", lambda e, tc=tc, ih=ih: e.scalar_tensor_tensor(
                tm(4, tc, ih * 512, 512), wtmp, 1.0, stmp, op0=ALU.add, op1=ALU.mult),
              waits=[(VE, v2)], inc=VE)
            ve_c = VE.n
            se_c = SE.n
            pump(2.5)
    mm_c_done = MM.n
    ve_c_done = VE.n

    se_w1s = cast_weight(w1s, RW0, pre_waits=[(MM, mm_c_done)], stage=cast_stage2)

    mark('C')
    def transpose_to_tm(src_si, dst_si, extra=()):
        for tc in range(NTC):
            bank = pt[tc % 2]
            w = bwait(bank)
            for dc in range(NDC):
                E("tensor", lambda e, b=bank, dc=dc, tc=tc: e.matmul(
                    b[:, dc * 128:(dc + 1) * 128], fm(src_si, dc, tc * 128, 128),
                    idb[:], is_transpose=True),
                  waits=(w if dc == 0 else []) + (list(extra) if (tc == 0 and dc == 0) else []),
                  inc=(MM if dc == 7 else None))
            mmv = MM.n
            E("scalar", lambda e, b=bank, tc=tc: e.copy(tm(dst_si, tc, 0, D), b[:]),
              waits=[(MM, mmv)], inc=SE)
            bset(bank, SE, SE.n)
            pump(1.5)
        return SE.n

    se_d2tm = transpose_to_tm(0, 2, extra=[(VE, ve_c_done), (SE, se_w1s)])

    mark('E')
    gst = [wa_f[:, 4096 + i * 512: 4096 + (i + 1) * 512] for i in range(2)]
    gkey = {}

    def g_mm(lhs_si, rhs_si, dst_dram, extra=()):
        for oc in range(NDC):
            for dh in range(2):
                bank = next_bank()
                w = bwait(bank)
                for tc in range(NTC):
                    E("tensor", lambda e, b=bank, tc=tc, oc=oc, dh=dh: e.matmul(
                        b[:], tm(lhs_si, tc, oc * 128, 128),
                        tm(rhs_si, tc, dh * 512, 512),
                        start=(tc == 0), stop=(tc == 15)),
                      waits=(w if tc == 0 else []) + (list(extra) if (oc == 0 and dh == 0 and tc == 0) else []),
                      inc=(MM if tc == 15 else None))
                mmv = MM.n
                i = (oc * 2 + dh) % 2
                st = gst[i]
                wg_ = gkey.get(str(i))
                E("vector", lambda e, b=bank, s=st: e.tensor_copy(s, b[:]),
                  waits=[(MM, mmv)] + ([wg_] if wg_ else []), inc=VE)
                vv = VE.n
                bset(bank, VE, vv)
                gkey[str(i)] = E("scalar", lambda e, s=st, oc=oc, dh=dh, d=dst_dram: e.dma_start(
                    out=d.ap()[oc * 128:(oc + 1) * 128, dh * 512:(dh + 1) * 512], in_=s),
                  waits=[(VE, vv)], inc=SC)
                pump(4.0)
        return SC.all()

    sc_g1 = g_mm(3, 2, g1_part, extra=[(SE, se_d2tm), (VE, ve_c_done)])

    mark('F')
    for tc in range(NTC):
        for eh in range(2):
            bank = next_bank()
            w = bwait(bank)
            for dc in range(NDC):
                E("tensor", lambda e, b=bank, dc=dc, tc=tc, eh=eh: e.matmul(
                    b[:], fm(0, dc, tc * 128, 128),
                    RW0[:, dc * 1024 + eh * 512: dc * 1024 + eh * 512 + 512],
                    start=(dc == 0), stop=(dc == 7)),
                  waits=(w if dc == 0 else []) + ([(SE, se_w1s)] if (tc == 0 and eh == 0 and dc == 0) else []),
                  inc=(MM if dc == 7 else None))
            mmv = MM.n
            E("vector", lambda e, b=bank, tc=tc, eh=eh: e.tensor_mul(
                tm(3, tc, eh * 512, 512), b[:], tm(4, tc, eh * 512, 512)),
              waits=[(MM, mmv)], inc=VE)
            bset(bank, VE, VE.n)
            pump(2.5)
    mm_d1_done = MM.n
    ve_d1_done = VE.n

    mark('G')
    se_ktm = transpose_to_tm(1, 0, extra=[(MM, mm_d1_done)])

    mark('H')
    sc_g0 = g_mm(0, 3, g0_part, extra=[(SE, se_ktm), (VE, ve_d1_done)])

    mark('I')
    E("gpsimd", lambda e: e.collective_compute(
        "AllReduce", ALU.add, replica_groups=[list(range(NCORES))],
        ins=[g1_part.ap().opt()], outs=[g1_sum.ap().opt()]),
      waits=sc_g1, inc=CC)
    cc_g1 = CC.n
    E("gpsimd", lambda e: e.collective_compute(
        "AllReduce", ALU.add, replica_groups=[list(range(NCORES))],
        ins=[g0_part.ap().opt()], outs=[g0_sum.ap().opt()]),
      waits=sc_g0, inc=CC)
    cc_g0 = CC.n

    mark('colls')
    gchunk = slots_f[1][:, 4096:5120]
    wchunk = slots_f[1][:, 5120:6144]
    ve_j = 0

    def w_update(gsum, wT_in, dst_bf, cc_wait):
        nonlocal_ve = [VE.n]
        for c in range(NDC):
            s1_ = E("sync", lambda e, c=c, g=gsum: e.dma_start(
                out=gchunk, in_=g.ap()[c * 128:(c + 1) * 128, :]),
              waits=[(CC, cc_wait), (VE, nonlocal_ve[0])], inc=SD)
            s2_ = E("sync", lambda e, c=c, w=wT_in: e.dma_start(
                out=wchunk, in_=w[c * 128:(c + 1) * 128, :]), inc=SD)
            E("vector", lambda e: e.tensor_scalar(
                gchunk, gchunk, 1.0, -1.0, op0=ALU.min, op1=ALU.max),
              waits=[s1_, (VE, nonlocal_ve[0])], inc=VE)
            v1_ = VE.n
            E("vector", lambda e: e.tensor_scalar(
                gchunk, gchunk, gal[:, 1:2], None, op0=ALU.mult),
              waits=[(VE, v1_), sc_gal], inc=VE)
            v2_ = VE.n
            E("vector", lambda e, c=c, d=dst_bf: e.scalar_tensor_tensor(
                d[:, c * 1024:(c + 1) * 1024], wchunk, gal[:, 0:1], gchunk,
                op0=ALU.mult, op1=ALU.add),
              waits=[(VE, v2_), s2_], inc=VE)
            nonlocal_ve[0] = VE.n
            pump(3.5)
        return VE.n

    W1nT_bf = slots[4][:, 0:8192]
    ve_w0n = w_update(g0_sum, w0T, RW1, cc_g0)
    ve_w1n = w_update(g1_sum, w1T, W1nT_bf, cc_g1)
    if DBG:
        E("scalar", lambda e: e.dma_start(out=dbg["dbg_w0nT"], in_=RW1),
          waits=[(VE, ve_w0n)], inc=SC)

    mark('J')
    sd_q = E("sync", lambda e: e.dma_start(
        out=slots[1][:].rearrange("p (c t) -> p c t", c=8),
        in_=q_dram.ap().rearrange("(c p) t -> p c t", p=128)),
      waits=sc_q_done + [(VE, ve_w1n)], inc=SD)

    mm_fm_set(RW1, 1, evac_to_slot(2, AF.Silu), extra_w=[sd_q, (VE, ve_w0n)])
    mm_h1_done = MM.n
    se_h1 = SE.n

    mm_fm_set(W1nT_bf, 2, evac_to_slot(3), extra_w=[(SE, se_h1), (VE, ve_w1n)])
    mm_h_done = MM.n
    cast_stage0 = [slots_f[0][:, 0:1024], slots_f[0][:, 1024:2048]]
    se_wout = cast_weight(woutT, RW1, pre_waits=[(MM, mm_h1_done)] + SC.all(),
                          stage=cast_stage0)
    mm_fm_set(RW1, 3, evac_to_slot(1), extra_w=[(SE, se_wout)])
    mm_att_done = MM.n
    se_att_done = SE.n
    if DBG:
        E("scalar", lambda e: e.dma_start(out=dbg["dbg_attn"], in_=slots[1][:]),
          waits=[(SE, se_att_done)], inc=SC)

    mark('K')
    se_atm = transpose_to_tm(1, 2, extra=[(SE, se_att_done)])

    sq2 = slots_f[1][:, 0:1024]
    ve_sq = 0
    se_sq = 0
    sc_dbgL = SC.all()
    for tc in range(NTC):
        E("vector", lambda e, tc=tc: e.reduce_sum(stat[:, tc:tc + 1], tm(2, tc, 0, D), axis=AX.X),
          waits=[(SE, se_atm)], inc=VE)
        E("scalar", lambda e, tc=tc: e.activation(sq2, tm(2, tc, 0, D), AF.Square),
          waits=[(SE, se_atm), (VE, ve_sq)] + sc_dbgL, inc=SE)
        sec = SE.n
        E("vector", lambda e, tc=tc: e.reduce_sum(stat2[:, tc:tc + 1], sq2, axis=AX.X),
          waits=[(SE, sec)], inc=VE)
        ve_sq = VE.n
        pump(3.0)
    ve_la = stats_chain(VE.n, stat, stat2, mcol, rcol)

    xb2 = [slots_f[1][:, 1024:2048], slots_f[1][:, 2048:3072]]
    y1b = [slots_f[1][:, 3072:4096], slots_f[1][:, 4096:5120]]
    lkey = {"x0": 0, "x1": 0}
    for tc in range(NTC):
        bx = xb2[tc % 2]
        by = y1b[tc % 2]
        sdc = E("sync", lambda e, tc=tc, b=bx: e.dma_start(
            out=b, in_=x_tm[tc * 128:(tc + 1) * 128, :]),
          waits=[(VE, lkey[f"x{tc % 2}"]), (VE, ve_la)], inc=SD)
        wy = lkey.get(f"y{tc % 2}")
        E("vector", lambda e, tc=tc, b=by: e.tensor_scalar(
            b, tm(2, tc, 0, D), mcol[:, tc:tc + 1], rcol[:, tc:tc + 1],
            op0=ALU.subtract, op1=ALU.mult),
          waits=([wy] if wy else []) + [(VE, VE.n)], inc=VE)
        v1 = VE.n
        E("vector", lambda e, bx=bx, by=by: e.tensor_add(by, by, bx),
          waits=[(VE, v1), sdc], inc=VE)
        v2 = VE.n
        E("vector", lambda e, tc=tc, by=by: e.reduce_sum(stat[:, tc:tc + 1], by, axis=AX.X),
          waits=[(VE, v2)], inc=VE)
        lkey[f"x{tc % 2}"] = VE.n
        E("scalar", lambda e, by=by: e.activation(sq2, by, AF.Square),
          waits=[(VE, v2), (VE, ve_sq)], inc=SE)
        sec = SE.n
        lkey[f"se{tc % 2}"] = sec
        E("vector", lambda e, tc=tc: e.reduce_sum(stat2[:, tc:tc + 1], sq2, axis=AX.X),
          waits=[(SE, sec)], inc=VE)
        ve_sq = VE.n
        lkey[f"y{tc % 2}"] = E("scalar", lambda e, tc=tc, by=by: e.dma_start(
            out=y1_dram.ap()[tc * 128:(tc + 1) * 128, :], in_=by),
          waits=[(VE, v2)], inc=SC)
        pump(5.0)
    sc_y1 = SC.all()
    ve_ly = stats_chain(VE.n, stat, stat2, mcol, rcol)

    ub = [slots[1][:, 10240:11264], slots[1][:, 11264:12288]]
    for tc in range(NTC):
        by = y1b[tc % 2]
        sdc = E("sync", lambda e, tc=tc, b=by: e.dma_start(
            out=b, in_=y1_dram.ap()[tc * 128:(tc + 1) * 128, :]),
          waits=sc_y1 + [(VE, lkey[f"x{tc % 2}"]),
                         (SE, lkey.get(f"se{tc % 2}", 0))], inc=SD)
        bu = ub[tc % 2]
        E("vector", lambda e, tc=tc, b=by, u=bu: e.tensor_scalar(
            u, b, mcol[:, tc:tc + 1], rcol[:, tc:tc + 1], op0=ALU.subtract, op1=ALU.mult),
          waits=[sdc, (VE, ve_ly), lkey[f"y{tc % 2}"]], inc=VE)
        v1 = VE.n
        lkey[f"x{tc % 2}"] = v1
        E("vector", lambda e, tc=tc, u=bu: e.reduce_sum(stat[:, tc:tc + 1], u, axis=AX.X),
          waits=[(VE, v1)], inc=VE)
        E("scalar", lambda e, u=bu: e.activation(sq2, u, AF.Square),
          waits=[(VE, v1), (VE, ve_sq)], inc=SE)
        sec = SE.n
        E("vector", lambda e, tc=tc: e.reduce_sum(stat2[:, tc:tc + 1], sq2, axis=AX.X),
          waits=[(SE, sec)], inc=VE)
        ve_sq = VE.n
        lkey[f"y{tc % 2}"] = E("scalar", lambda e, tc=tc, u=bu: e.dma_start(
            out=ublk.ap()[:, tc * 128:(tc + 1) * 128, :].rearrange("g p d -> p g d"),
            in_=u.rearrange("p (g d) -> p g d", g=8)),
          waits=[(VE, v1)], inc=SC)
        pump(5.0)
    sc_ublk = SC.all()
    ve_lu = stats_chain(VE.n, stat, stat2, muU, rU)
    E("vector", lambda e: e.tensor_mul(rmU[:], rU[:], muU[:]), waits=[(VE, ve_lu)], inc=VE)
    ve_rmU = VE.n

    mark('L')
    for dc in range(NDC):
        E("scalar", lambda e, dc=dc: e.dma_start(
            out=fm(1, dc, 0, TOK), in_=ublk.ap()[dc], transpose=True),
          waits=sc_ublk, inc=SC)
    sc_ufm = SC.all()

    t_wr = E("sync", lambda e: e.dma_start(
        out=wr_sb[:].rearrange("p (c e) -> p c e", c=8),
        in_=wr_i.rearrange("(c p) e -> p c e", p=128)), inc=SD)
    t_gam = E("sync", lambda e: e.dma_start(out=gam_sb[:], in_=gamg_i), inc=SD)
    t_bet = E("sync", lambda e: e.dma_start(out=bet_sb[:], in_=betg_i), inc=SD)
    t_br = E("sync", lambda e: e.dma_start(out=br_sb[:], in_=br_i), inc=SD)
    sd_wr = [t_wr, t_gam, t_bet, t_br]
    E("vector", lambda e: e.memset(onesb[:], 1.0), inc=VE)
    ve_ones = VE.n
    for c in range(NDC):
        E("vector", lambda e, c=c: e.tensor_scalar(
            wrp[:, c * 8:(c + 1) * 8], wr_sb[:, c * 8:(c + 1) * 8],
            gam_sb[:, c:c + 1], None, op0=ALU.mult),
          waits=sd_wr, inc=VE)
        E("vector", lambda e, c=c: e.tensor_scalar(
            wrb[:, c * 8:(c + 1) * 8], wr_sb[:, c * 8:(c + 1) * 8],
            bet_sb[:, c:c + 1], None, op0=ALU.mult), inc=VE)
    ve_wrp = VE.n
    w = bwait(pr)
    for c in range(NDC):
        E("tensor", lambda e, c=c: e.matmul(
            pr[0:1, 0:8], onesb[:], wrp[:, c * 8:(c + 1) * 8],
            start=(c == 0), stop=(c == 7)),
          waits=(w if c == 0 else []) + ([(VE, ve_wrp), (VE, ve_ones)] if c == 0 else []),
          inc=(MM if c == 7 else None))
    mm1 = MM.n
    E("vector", lambda e: e.tensor_copy(c1r[:], pr[0:1, 0:8]), waits=[(MM, mm1)], inc=VE)
    ve_c1 = VE.n
    bset(pr, VE, ve_c1)
    w = bwait(pr)
    for c in range(NDC):
        E("tensor", lambda e, c=c: e.matmul(
            pr[0:1, 0:8], onesb[:], wrb[:, c * 8:(c + 1) * 8],
            start=(c == 0), stop=(c == 7)),
          waits=(w if c == 0 else []), inc=(MM if c == 7 else None))
    mm2 = MM.n
    E("vector", lambda e: e.tensor_copy(c2r[:], pr[0:1, 0:8]), waits=[(MM, mm2)], inc=VE)
    v1 = VE.n
    bset(pr, VE, v1)
    E("vector", lambda e: e.tensor_add(c2r[:], c2r[:], br_sb[:]), waits=[(VE, v1), t_br], inc=VE)
    ve_c2 = VE.n
    tc1 = E("scalar", lambda e: e.dma_start(out=cbc_d.ap()[0:1, 0:8], in_=c1r[:]),
      waits=[(VE, ve_c1)], inc=SC)
    tc2 = E("scalar", lambda e: e.dma_start(out=cbc_d.ap()[0:1, 8:16], in_=c2r[:]),
      waits=[(VE, ve_c2)], inc=SC)
    sc_c12 = E("scalar", lambda e: e.dma_start(out=c12bc[:], in_=cbc_d.ap().broadcast_to([128, 16])),
      waits=[tc1, tc2], inc=SC)

    for tc in range(NTC):
        w = bwait(pr)
        for hc in range(NDC):
            E("tensor", lambda e, hc=hc, tc=tc: e.matmul(
                pr[:, 0:8], fm(1, hc, tc * 128, 128), wrp[:, hc * 8:(hc + 1) * 8],
                start=(hc == 0), stop=(hc == 7)),
              waits=(w if hc == 0 else []) + (sc_ufm if (tc == 0 and hc == 0) else []),
              inc=(MM if hc == 7 else None))
        mmv = MM.n
        E("vector", lambda e, tc=tc: e.tensor_scalar(
            lgt[:], pr[:, 0:8], rU[:, tc:tc + 1], None, op0=ALU.mult),
          waits=[(MM, mmv), (VE, ve_rmU), sc_c12], inc=VE)
        v1 = VE.n
        bset(pr, VE, v1)
        E("vector", lambda e, tc=tc: e.scalar_tensor_tensor(
            lg2[:], c12bc[:, 0:8], rmU[:, tc:tc + 1], lgt[:],
            op0=ALU.mult, op1=ALU.subtract),
          waits=[(VE, v1)], inc=VE)
        v2 = VE.n
        E("vector", lambda e: e.tensor_sub(lg3[:], c12bc[:, 8:16], lg2[:]),
          waits=[(VE, v2)], inc=VE)
        v3 = VE.n
        E("vector", lambda e: e.reduce_max(sm[0][:], lg3[:], axis=AX.X),
          waits=[(VE, v3)], inc=VE)
        v4 = VE.n
        E("vector", lambda e: e.tensor_scalar(sm[1][:], sm[0][:], -1.0, None, op0=ALU.mult),
          waits=[(VE, v4)], inc=VE)
        v5 = VE.n
        E("scalar", lambda e: e.activation(expv[:], lg3[:], AF.Exp, bias=sm[1][:]),
          waits=[(VE, v5)], inc=SE)
        s1_ = SE.n
        E("vector", lambda e: e.reduce_sum(sm[2][:], expv[:], axis=AX.X),
          waits=[(SE, s1_)], inc=VE)
        v6 = VE.n
        E("vector", lambda e: e.reciprocal(sm[3][:], sm[2][:]), waits=[(VE, v6)], inc=VE)
        v7 = VE.n
        E("vector", lambda e, tc=tc: e.tensor_mul(pcc[:, tc:tc + 1], expv[:, 0:1], sm[3][:]),
          waits=[(VE, v7)], inc=VE)
        E("vector", lambda e: e.tensor_sub(sm[4][:], sm[2][:], expv[:, 0:1]),
          waits=[(VE, v7)], inc=VE)
        v8 = VE.n
        E("vector", lambda e, tc=tc: e.tensor_mul(ppc[:, tc:tc + 1], sm[4][:], sm[3][:]),
          waits=[(VE, v8)], inc=VE)
        pump(2.0)
    ve_pc = VE.n
    tp1 = E("scalar", lambda e: e.dma_start(out=pcc_d.ap(), in_=pcc[:]),
      waits=[(VE, ve_pc)], inc=SC)
    tp2 = E("scalar", lambda e: e.dma_start(out=ppc_d.ap(), in_=ppc[:]),
      inc=SC)
    mark('M')
    cst_f = [slots_f[3][:, 0:4096], slots_f[3][:, 4096:8192]]
    cst_b = [wa_b[:, 8192:12288], wa_b[:, 12288:16384]]
    ckey = {"0": (0, None), "1": (0, None)}

    def cast_chunk(i, n, src_ap, dst_ap, view=None):
        sf = cst_f[i % 2][:, 0:n]
        sb = cst_b[i % 2][:, 0:n]
        se_prev, sc_prev = ckey[str(i % 2)]
        sdc = E("sync", lambda e, s=sf, src=src_ap: e.dma_start(out=s, in_=src),
          waits=[(SE, se_prev), (MM, mm_att_done)], inc=SD)
        E("scalar", lambda e, o=sb, s=sf: e.copy(o, s),
          waits=[sdc, (MM, mm_att_done)] + ([sc_prev] if sc_prev else []), inc=SE)
        sev = SE.n
        tok = E("scalar", lambda e, o=dst_ap, s=sb, v=view: e.dma_start(
            out=o, in_=(v(s) if v else s)),
          waits=[(SE, sev)], inc=SC)
        ckey[str(i % 2)] = (sev, tok)
        pump(1.5)

    for hc in range(NDC):
        cast_chunk(hc, 4096,
                   w1ff[hc * 128:(hc + 1) * 128, :],
                   w1t_loc.ap().rearrange("(f h) p a -> h p f a", h=8)[hc],
                   view=lambda t: t.rearrange("p (f a) -> p f a", a=128))
    sc_w1cast = SC.all()
    for r in range(32):
        cast_chunk(r, 1024,
                   w2ff[r * 128:(r + 1) * 128, :],
                   w2bf_loc.ap()[r * 128:(r + 1) * 128, :])
    sc_w2cast = SC.all()

    mark('cast')
    wkey = {}

    def slab(name, i, off):
        return wa_b[:, off + (i % 2) * 1024: off + (i % 2) * 1024 + 1024]

    for sl in range(4):
        for ft in range(32):
            w1sb = slab("w1", ft, 0)
            ts1 = E("sync", lambda e, ft=ft, b=w1sb: e.dma_start(
                out=b.rearrange("p (h f) -> p h f", h=8),
                in_=w1t_loc.ap().rearrange("(f h) p a -> f p h a", h=8)[ft]),
              waits=[(MM, wkey.get(f"w1{ft % 2}", 0))] + (sc_w1cast if sl == 0 else []),
              inc=SD)
            bankA = next_bank()
            wA = bwait(bankA)
            for hc in range(NDC):
                E("tensor", lambda e, b=bankA, hc=hc, sl=sl, s=w1sb: e.matmul(
                    b[:], s[:, hc * 128:(hc + 1) * 128], fm(1, hc, sl * 512, 512),
                    start=(hc == 0), stop=(hc == 7)),
                  waits=((wA + [ts1, (MM, mm_att_done)]) if hc == 0 else []),
                  inc=(MM if hc == 7 else None))
            mm_pre = MM.n
            wkey[f"w1{ft % 2}"] = mm_pre
            E("scalar", lambda e, b=bankA, ft=ft: e.activation(
                slots[2][:, ft * 512:(ft + 1) * 512], b[:], AF.Gelu_apprx_tanh,
                bias=b1gs[:, ft:ft + 1]),
              waits=[(MM, mm_pre)], inc=SE)
            bset(bankA, SE, SE.n)
            pump(3.0)
        se_acts = SE.n
        mark(f"Oc{sl}a")
        for half in range(2):
            banks = [pm[i] for i in range(4)]
            bw = [bwait(bk) for bk in banks]
            for fc in range(32):
                w2sb = slab("w2", fc, 4096)
                sd_w2 = E("sync", lambda e, fc=fc, b=w2sb: e.dma_start(
                    out=b, in_=w2bf_loc.ap()[fc * 128:(fc + 1) * 128, :]),
                  waits=[(MM, wkey.get(f"w2{fc % 2}", 0))]
                        + (sc_w2cast if (sl == 0 and half == 0) else []),
                  inc=SD)
                for h4 in range(4):
                    hcg = half * 4 + h4
                    E("tensor", lambda e, b=banks[h4], fc=fc, hcg=hcg, s=w2sb, sl=sl: e.matmul(
                        b[:], s[:, hcg * 128:(hcg + 1) * 128],
                        slots[2][:, fc * 512:(fc + 1) * 512],
                        start=(fc == 0), stop=(fc == 31)),
                      waits=(([sd_w2, (SE, se_acts)] + bw[h4]) if fc == 0 else ([sd_w2] if h4 == 0 else [])),
                      inc=(MM if h4 == 3 else None))
                wkey[f"w2{fc % 2}"] = MM.n
            mm_y = MM.n
            for h4 in range(4):
                hcg = half * 4 + h4
                E("vector", lambda e, b=banks[h4], hcg=hcg, sl=sl: e.tensor_scalar(
                    slots[4][:, sl * 4096 + hcg * 512: sl * 4096 + hcg * 512 + 512],
                    b[:], b2gs[:, hcg:hcg + 1], None, op0=ALU.add),
                  waits=[(MM, mm_y)], inc=VE)
                bset(banks[h4], VE, VE.n)
        mark(f"Oc{sl}y")

    pump_flush()
    ve_srch = VE.n
    sc_thrf = sstate["sc_thrf"]

    tg1 = E("gpsimd", lambda e: e.dma_start(
        out=pcd1.ap().rearrange("a (c p) -> (a c) p", p=128),
        in_=pcc_d.ap().rearrange("p c -> c p"),
        allow_slow_non_contiguous=True),
      waits=[tp1, tp2], inc=GD)
    tg2 = E("gpsimd", lambda e: e.dma_start(
        out=pcd2.ap().rearrange("a (c p) -> (a c) p", p=128),
        in_=ppc_d.ap().rearrange("p c -> c p"),
        allow_slow_non_contiguous=True),
      inc=GD)
    Pc = sbig_b[:, 0:2048]
    Pp = sbig_b[:, 2048:4096]
    tP1 = E("scalar", lambda e: e.dma_start(out=Pc, in_=pcd1.ap().broadcast_to([128, TOK])),
      waits=[tg1, tg2, (VE, ve_srch)], inc=SC)
    tP2 = E("scalar", lambda e: e.dma_start(out=Pp, in_=pcd2.ap().broadcast_to([128, TOK])),
      inc=SC)
    sc_P = [tP1, tP2]
    if DBG:
        E("scalar", lambda e: e.dma_start(out=dbg["dbg_pc"][0:1, :], in_=pcd1.ap()),
          waits=[tg1, tg2], inc=SC)
        E("scalar", lambda e: e.dma_start(out=dbg["dbg_pc"][1:2, :], in_=pcd2.ap()),
          inc=SC)

    def pert_b(src, absrc, dst, w, thc_b, thc_t):
        E("vector", lambda e: e.tensor_scalar(
            tb1[:, 0:w], absrc[:, 0:w], thrf[:, thc_b:thc_b + 1], None, op0=ALU.is_le),
          waits=sc_thrf + [(VE, VE.n)], inc=VE)
        E("vector", lambda e: e.tensor_scalar(
            tb2[:, 0:w], absrc[:, 0:w], thrf[:, thc_t:thc_t + 1], None, op0=ALU.is_ge),
          waits=[(VE, VE.n)], inc=VE)
        E("vector", lambda e: e.tensor_sub(tb1[:, 0:w], tb1[:, 0:w], tb2[:, 0:w]),
          waits=[(VE, VE.n)], inc=VE)
        E("vector", lambda e: e.tensor_scalar(tb2[:, 0:w], src[:, 0:w], 0.8, None, op0=ALU.mult),
          waits=[(VE, VE.n)], inc=VE)
        E("vector", lambda e: e.tensor_mul(tb1[:, 0:w], tb2[:, 0:w], tb1[:, 0:w]),
          waits=[(VE, VE.n)], inc=VE)
        E("vector", lambda e: e.tensor_add(dst[:, 0:w], src[:, 0:w], tb1[:, 0:w]),
          waits=[(VE, VE.n)], inc=VE)
        return VE.n

    pert_b(b1gs, b1a, pb1g, 32, 4, 5)
    ve_pb1 = VE.n
    pert_b(b2gs, b2a, pb2g, 8, 6, 7)
    ve_pb2 = VE.n

    dg_f32 = slots_f[0][:, 0:4096]
    dg_A = slots_f[0][:, 4096:8192]
    dg_B = slots[2][:, 0:4096]

    def dgen(slice_in, thb, tht, store_out, store_view):
        sc_prev = SC.all()
        sdc = E("sync", lambda e: e.dma_start(out=dg_f32, in_=slice_in),
          waits=[(VE, VE.n), (SE, SE.n)] + sc_prev, inc=SD)
        E("vector", lambda e: e.tensor_scalar(
            dg_A, dg_f32, thrf[:, thb:thb + 1], None, op0=ALU.is_le),
          waits=[sdc, (MM, MM.n)] + sc_thrf + sc_prev, inc=VE)
        E("vector", lambda e: e.tensor_scalar(
            dg_B, dg_f32, nthrf[:, thb:thb + 1], None, op0=ALU.is_ge),
          waits=[(VE, VE.n), (MM, MM.n)], inc=VE)
        E("vector", lambda e: e.tensor_mul(dg_A, dg_A, dg_B), waits=[(VE, VE.n)], inc=VE)
        E("vector", lambda e: e.tensor_scalar(
            dg_B, dg_f32, thrf[:, tht:tht + 1], None, op0=ALU.is_ge),
          waits=[(VE, VE.n)], inc=VE)
        E("vector", lambda e: e.tensor_sub(dg_A, dg_A, dg_B), waits=[(VE, VE.n)], inc=VE)
        E("vector", lambda e: e.tensor_scalar(
            dg_B, dg_f32, nthrf[:, tht:tht + 1], None, op0=ALU.is_le),
          waits=[(VE, VE.n)], inc=VE)
        E("vector", lambda e: e.tensor_sub(dg_A, dg_A, dg_B), waits=[(VE, VE.n)], inc=VE)
        E("vector", lambda e: e.tensor_mul(dg_A, dg_A, dg_f32), waits=[(VE, VE.n)], inc=VE)
        E("vector", lambda e: e.tensor_scalar(dg_A, dg_A, 0.8, None, op0=ALU.mult),
          waits=[(VE, VE.n)], inc=VE)
        E("vector", lambda e: e.tensor_add(dg_B, dg_A, dg_f32), waits=[(VE, VE.n)], inc=VE)
        ve_p = VE.n
        t = E("scalar", lambda e: e.dma_start(out=store_out, in_=store_view(dg_B)),
          waits=[(VE, ve_p)], inc=SC)
        return [t]

    sc_dg1 = dgen(
        w1sl, 0, 1,
        w1p_in.ap().rearrange("f p a -> p f a"),
        lambda t: t.rearrange("p (f a) -> p f a", a=128))
    sc_dg2 = dgen(
        w2sl, 2, 3,
        pw2_in.ap().rearrange("(p a) d -> p a d", p=128),
        lambda t: t.rearrange("p (a d) -> p a d", d=D))
    ve_dgen = VE.n

    mark('dgen')
    E("gpsimd", lambda e: e.collective_compute(
        "AllGather", ALU.bypass, replica_groups=[list(range(NCORES))],
        ins=[w1p_in.ap().opt()], outs=[w1p_d.ap().opt()]),
      waits=sc_dg1, inc=CC)
    E("gpsimd", lambda e: e.collective_compute(
        "AllGather", ALU.bypass, replica_groups=[list(range(NCORES))],
        ins=[pw2_in.ap().opt()], outs=[pw2_d.ap().opt()]),
      waits=sc_dg2, inc=CC)
    cc_ag = CC.n

    mark('colls')
    y1t = slots_f[0][:, 0:4096]
    wkey["y1t"] = ve_dgen
    for sl in range(4):
        sd_y1t = E("sync", lambda e, sl=sl: e.dma_start(
            out=y1t.rearrange("p (c d) -> p c d", c=4),
            in_=y1_dram.ap()[sl * 512:(sl + 1) * 512, :].rearrange("(c p) d -> p c d", p=128)),
          waits=sc_y1 + [(VE, wkey.get("y1t", 0))], inc=SD)
        for ft in range(32):
            w1sb = slab("w1", ft, 0)
            ts1 = E("sync", lambda e, ft=ft, b=w1sb: e.dma_start(
                out=b.rearrange("p (h f) -> p h f", h=8),
                in_=w1p_d.ap().rearrange("(r t) p f -> r t p f", t=32)[:, ft].rearrange("r p f -> p r f")),
              waits=[(CC, cc_ag), (MM, wkey.get(f"w1{ft % 2}", 0))], inc=SD)
            bankA = next_bank()
            wA = bwait(bankA)
            for hc in range(NDC):
                E("tensor", lambda e, b=bankA, hc=hc, sl=sl, s=w1sb: e.matmul(
                    b[:], s[:, hc * 128:(hc + 1) * 128], fm(1, hc, sl * 512, 512),
                    start=(hc == 0), stop=(hc == 7)),
                  waits=((wA + [ts1]) if hc == 0 else []),
                  inc=(MM if hc == 7 else None))
            mm_pre = MM.n
            wkey[f"w1{ft % 2}"] = mm_pre
            E("scalar", lambda e, b=bankA, ft=ft: e.activation(
                slots[2][:, ft * 512:(ft + 1) * 512], b[:], AF.Gelu_apprx_tanh,
                bias=pb1g[:, ft:ft + 1]),
              waits=[(MM, mm_pre), (VE, ve_pb1)], inc=SE)
            bset(bankA, SE, SE.n)
        se_acts = SE.n
        mark(f"Op{sl}a")
        for half in range(2):
            banks = [pm[i] for i in range(4)]
            bw = [bwait(bk) for bk in banks]
            for fc in range(32):
                w2sb = slab("w2", fc, 4096)
                sd_w2 = E("sync", lambda e, fc=fc, b=w2sb: e.dma_start(
                    out=b, in_=pw2_d.ap()[fc * 128:(fc + 1) * 128, :]),
                  waits=[(CC, cc_ag), (MM, wkey.get(f"w2{fc % 2}", 0))], inc=SD)
                for h4 in range(4):
                    hcg = half * 4 + h4
                    E("tensor", lambda e, b=banks[h4], fc=fc, hcg=hcg, s=w2sb, sl=sl: e.matmul(
                        b[:], s[:, hcg * 128:(hcg + 1) * 128],
                        slots[2][:, fc * 512:(fc + 1) * 512],
                        start=(fc == 0), stop=(fc == 31)),
                      waits=(([sd_w2, (SE, se_acts)] + bw[h4]) if fc == 0 else ([sd_w2] if h4 == 0 else [])),
                      inc=(MM if h4 == 3 else None))
                wkey[f"w2{fc % 2}"] = MM.n
            mm_y = MM.n
            for h4 in range(4):
                hcg = half * 4 + h4
                E("vector", lambda e, b=banks[h4], hcg=hcg: e.tensor_scalar(
                    slots[0][:, 8192 + hcg * 512: 8192 + hcg * 512 + 512],
                    b[:], pb2g[:, hcg:hcg + 1], None, op0=ALU.add),
                  waits=[(MM, mm_y), (VE, ve_pb2)], inc=VE)
                bset(banks[h4], VE, VE.n)
        mark(f"Op{sl}y")
        ctmp = wa_f[:, 3072:3584]
        ve_comb = 0
        for hcg in range(NDC):
            E("vector", lambda e, hcg=hcg, sl=sl: e.tensor_tensor(
                ctmp, slots[4][:, sl * 4096 + hcg * 512: sl * 4096 + hcg * 512 + 512],
                Pc[:, sl * 512:(sl + 1) * 512], op=ALU.mult),
              waits=[(VE, VE.n), (VE, ve_comb)] + sc_P
                    + ([wkey["ob"]] if "ob" in wkey else []), inc=VE)
            v1 = VE.n
            E("vector", lambda e, hcg=hcg, sl=sl: e.tensor_tensor(
                wa_f[:, 3584:4096], slots[0][:, 8192 + hcg * 512: 8192 + hcg * 512 + 512],
                Pp[:, sl * 512:(sl + 1) * 512], op=ALU.mult),
              waits=[(VE, v1)], inc=VE)
            v2 = VE.n
            E("vector", lambda e, hcg=hcg, sl=sl: e.tensor_add(
                slots[4][:, sl * 4096 + hcg * 512: sl * 4096 + hcg * 512 + 512],
                ctmp, wa_f[:, 3584:4096]),
              waits=[(VE, v2)], inc=VE)
            ve_comb = VE.n
        ve_fo = VE.n
        mark(f"Op{sl}c")
        for tcl in range(4):
            bank = pt[tcl % 2]
            w = bwait(bank)
            for hcg in range(NDC):
                E("tensor", lambda e, b=bank, hcg=hcg, tcl=tcl, sl=sl: e.matmul(
                    b[:, hcg * 128:(hcg + 1) * 128],
                    slots[4][:, sl * 4096 + hcg * 512 + tcl * 128: sl * 4096 + hcg * 512 + tcl * 128 + 128],
                    idb[:], is_transpose=True),
                  waits=(w + [(VE, ve_fo)] if hcg == 0 else []),
                  inc=(MM if hcg == 7 else None))
            mmv = MM.n
            obsq = wa_f[:, 5120:6144]
            E("scalar", lambda e, b=bank, o=obsq: e.activation(o, b[:], AF.Square),
              waits=[(MM, mmv), (VE, ve_sq)], inc=SE)
            s1_ = SE.n
            E("vector", lambda e, b=bank: e.reduce_sum(sm[0][:], b[:], axis=AX.X),
              waits=[(MM, mmv), (SE, s1_)], inc=VE)
            E("vector", lambda e, o=obsq: e.reduce_sum(sm[1][:], o, axis=AX.X),
              waits=[(SE, s1_)], inc=VE)
            v1 = VE.n
            E("vector", lambda e: e.tensor_scalar(sm[2][:], sm[0][:], 1.0 / D, None, op0=ALU.mult),
              waits=[(VE, v1)], inc=VE)
            v2 = VE.n
            E("vector", lambda e: e.tensor_mul(sm[3][:], sm[2][:], sm[2][:]),
              waits=[(VE, v2)], inc=VE)
            v3 = VE.n
            E("vector", lambda e: e.scalar_tensor_tensor(
                sm[4][:], sm[1][:], 1.0 / D, sm[3][:], op0=ALU.mult, op1=ALU.subtract),
              waits=[(VE, v3)], inc=VE)
            v4 = VE.n
            E("scalar", lambda e: e.activation(sm[5][:], sm[4][:], AF.Sqrt, bias=epsb[:]),
              waits=[(VE, v4)], inc=SE)
            s2_ = SE.n
            E("vector", lambda e: e.reciprocal(sm[5][:], sm[5][:]), waits=[(SE, s2_)], inc=VE)
            v5 = VE.n
            ob = wa_f[:, 4096:5120]
            E("vector", lambda e, b=bank: e.tensor_scalar(
                ob, b[:], sm[2][:], sm[5][:], op0=ALU.subtract, op1=ALU.mult),
              waits=[(VE, v5), (VE, ve_comb)] + ([wkey["ob"]] if "ob" in wkey else []), inc=VE)
            v6 = VE.n
            E("vector", lambda e, tcl=tcl: e.tensor_add(
                ob, ob, y1t[:, tcl * 1024:(tcl + 1) * 1024]),
              waits=[(VE, v6), sd_y1t], inc=VE)
            v7 = VE.n
            ve_sq = v7
            wkey["y1t"] = v7
            bset(bank, VE, v7)
            wkey["ob"] = E("scalar", lambda e, sl=sl, tcl=tcl: e.dma_start(
                out=out_tm[sl * 512 + tcl * 128: sl * 512 + tcl * 128 + 128, :], in_=ob),
              waits=[(VE, v7)], inc=SC)

    E("sync", lambda e: e.nop(),
      waits=SC.all() + SD.all() + GD.all())

    mark("end")
    globals()["MARKS"] = dict(marks)
    _stop = stop_at if stop_at is not None else STOP_AT
    if _stop is not None:
        cut = marks[_stop]
        for k in plan:
            plan[k] = [f for f in plan[k] if f._seq < cut]

    with ctx:
        block = ctx.enter_context(nc.Block())

        @block.sync
        def _(sync):
            for fn in plan["sync"]:
                fn(sync)

        @block.scalar
        def _(s):
            for fn in plan["scalar"]:
                fn(s)

        @block.vector
        def _(v):
            for fn in plan["vector"]:
                fn(v)

        @block.tensor
        def _(t):
            for fn in plan["tensor"]:
                fn(t)

        @block.gpsimd
        def _(g):
            for fn in plan["gpsimd"]:
                fn(g)

    return nc


_NC_CACHE = {}


def _get_nc():
    key = STOP_AT
    if key not in _NC_CACHE:
        _NC_CACHE[key] = build_nc()
    return _NC_CACHE[key]


def kernel(**inputs):
    ii = inputs
    x = np.asarray(ii["L0_attn_x"], dtype=np.float32)

    def T(a):
        return np.ascontiguousarray(np.asarray(a, dtype=np.float32).T)

    wkT = T(ii["L0_attn_Wk"])
    wvT = T(ii["L0_attn_Wv"])
    wqT = T(ii["L0_attn_Wq"])
    woutT = T(ii["L0_attn_Wout"])
    w0T = T(ii["L0_attn_W0"])
    w1T = T(ii["L0_attn_W1"])
    w1s = np.ascontiguousarray(np.asarray(ii["L0_attn_W1"], dtype=np.float32))
    wg2 = np.ascontiguousarray(np.stack(
        [np.asarray(ii["L0_attn_Wg_decay"], np.float32)[0],
         np.asarray(ii["L0_attn_Wg_lr"], np.float32)[0]], axis=1))
    bg2 = np.array([[float(np.asarray(ii["L0_attn_bg_decay"]).ravel()[0]),
                     float(np.asarray(ii["L0_attn_bg_lr"]).ravel()[0])]], np.float32)
    w1f = np.ascontiguousarray(np.asarray(ii["L0_ffn_W1"], np.float32))
    w2f = np.ascontiguousarray(np.asarray(ii["L0_ffn_W2"], np.float32))
    b1 = np.asarray(ii["L0_ffn_b1"], np.float32).ravel()
    b2 = np.asarray(ii["L0_ffn_b2"], np.float32).ravel()
    gam = np.asarray(ii["L0_ffn_ln_gamma"], np.float32).ravel()
    bet = np.asarray(ii["L0_ffn_ln_beta"], np.float32).ravel()
    wr = np.ascontiguousarray(np.asarray(ii["L0_ffn_Wr"], np.float32))
    br = np.asarray(ii["L0_ffn_br"], np.float32).reshape(1, 8)

    b1g = np.ascontiguousarray(b1.reshape(32, 128).T)
    b2g = np.ascontiguousarray(b2.reshape(8, 128).T)
    gamg = np.ascontiguousarray(gam.reshape(8, 128).T)
    betg = np.ascontiguousarray(bet.reshape(8, 128).T)

    import ml_dtypes
    idb = np.eye(128, dtype=np.float32).astype(ml_dtypes.bfloat16)
    idf = np.eye(128, dtype=np.float32)

    nc = _get_nc()
    in_maps = []
    for c in range(NCORES):
        m = {
            "x_tm": np.ascontiguousarray(x[c]),
            "wkT": wkT, "wvT": wvT, "wqT": wqT, "w0T": w0T, "w1T": w1T,
            "w1s": w1s, "woutT": woutT, "wg2": wg2, "bg2": bg2,
            "b1g": b1g, "b2g": b2g, "gamg": gamg, "betg": betg,
            "wr": wr, "br": br,
            "w1sl": np.ascontiguousarray(w1f[c * 128:(c + 1) * 128, :]),
            "w1ff": w1f, "w2ff": w2f,
            "w2sl": np.ascontiguousarray(
                w2f[c * 512:(c + 1) * 512, :].reshape(128, 4096)),
            "idb": idb,
        }
        in_maps.append(m)

    res = run_bass_kernel_spmd(nc, in_maps, core_ids=list(range(NCORES)))
    out = np.stack([r["out_tm"] for r in res.results], axis=0)
    if DBG:
        kernel.last_results = res.results
    return out
